# version 14
# baseline (speedup 1.0000x reference)
import numpy as np
import concourse.bass as bass
import concourse.mybir as mybir
from concourse.bass_utils import run_bass_kernel_spmd

F32 = mybir.dt.float32
BF16 = mybir.dt.bfloat16
Alu = mybir.AluOpType
Act = mybir.ActivationFunctionType
AX = mybir.AxisListType.X

DM = 2048
LSEQ = 2048
NST = 16
NE = 8
NJ = 16
TS = 256
NCH = LSEQ // TS
TBT = 512
NTBG = LSEQ // TBT
EPS = 1e-5
RN = 1.0 / DM


class Sem:
    def __init__(self, nc, name):
        self._cm = nc.semaphore(name)
        self.h = self._cm.__enter__()
        self.v = 0


def build_nc():
    nc = bass.Bass("TRN2", target_bir_lowering=False)

    x_h = nc.dram_tensor("x", [LSEQ, DM], F32, kind="ExternalInput").ap()
    wxp_h = nc.dram_tensor("wxp", [16, 128, 2048], F32, kind="ExternalInput").ap()
    wbc_h = nc.dram_tensor("wbc", [16, 128, 32], F32, kind="ExternalInput").ap()
    alog_h = nc.dram_tensor("alog", [128, 256], F32, kind="ExternalInput").ap()
    dvec_h = nc.dram_tensor("dvec", [128, 16], F32, kind="ExternalInput").ap()
    wg_h = nc.dram_tensor("wg", [128, 128], F32, kind="ExternalInput").ap()
    b_h = nc.dram_tensor("bias", [8, 2048], F32, kind="ExternalInput").ap()
    w5_h = nc.dram_tensor("w5", [16, 8, 128, 2048], F32, kind="ExternalInput").ap()
    y_h = nc.dram_tensor("y", [LSEQ, DM], F32, kind="ExternalOutput").ap()
    xp_s = nc.dram_tensor("xps", [LSEQ, DM], F32, kind="Internal").ap()
    ln2_s = nc.dram_tensor("ln2s", [NCH, 128, NJ * TS], BF16, kind="Internal").ap()

    def sb(name, shape, dt=F32):
        return nc.alloc_sbuf_tensor(name, shape, dt).ap()

    A_raw = sb("A_raw", [128, 256])
    A_sb = sb("A_sb", [128, 256])
    D_sb = sb("D_sb", [128, 16])
    wg_bf = sb("wg_bf", [128, 128], BF16)
    wxpbc_bf = sb("wxpbc", [128, 512], BF16)
    identf = sb("identf", [128, 128])
    identb = sb("identb", [128, 128], BF16)
    carry = sb("carry", [128, 512])
    masks = sb("masks", [128, 128])
    gate = sb("gate", [128, 16])
    flagb = sb("flagb", [128, 16])
    tmp16 = sb("tmp16", [128, 16])
    stats = sb("stats", [128, 64])
    lg_sb = sb("lg_sb", [8, 256])
    bc_sb = sb("bc_sb", [32, 256])
    bb_sb = sb("bb_sb", [128, 512])
    cb_sb = sb("cb_sb", [128, 512])
    wf32 = sb("wf32", [128, 2 * 2048])
    consts = sb("consts", [128, 8])

    xbuf0 = sb("xbuf0", [128, 2 * DM])
    xbuf1 = sb("xbuf1", [128, 2 * DM])
    tmbuf = sb("tmbuf", [128, NJ * (TS + 1)])
    fmbuf = sb("fmbuf", [128, NJ * TS])
    ln1fb = sb("ln1fb", [128, NJ * TS], BF16)
    dtb = sb("dtb", [128, NJ * TS])
    dtxb = sb("dtxb", [128, NJ * TS])
    ybuf = sb("ybuf", [128, NJ * TS])
    abuf = sb("abuf", [128, NJ * (TS + 1)])
    ubuf = sb("ubuf", [128, NJ * (TS + 1)])
    hbuf = sb("hbuf", [128, NJ * (TS + 1)])
    wxpK = sb("wxpK", [128, 2 * 2048], BF16)
    masksT = wxpK[0:8, 0:2048]
    b_bf = wxpK[0:8, 2048:4096]

    xe = [dtb[:].bitcast(BF16), dtxb[:].bitcast(BF16),
          ybuf[:].bitcast(BF16), abuf[:].bitcast(BF16)[:, 0:8192]]
    ln2TBG = ubuf[:].bitcast(BF16)[:, 0:8192]
    outFM_lo = hbuf
    outFM_hi = fmbuf
    wstage = xbuf0
    outtile = [xbuf1[:][:, 0:DM], xbuf1[:][:, DM:2 * DM]]
    xptile = [tmbuf[:][:, 0:DM], tmbuf[:][:, DM:2 * DM]]
    otmp = [ln1fb[:].bitcast(F32)[:, 0:512], ln1fb[:].bitcast(F32)[:, 512:1024]]
    meb_bf = ln1fb[:][:, 2048:2560]

    def wslot(i):
        v = wstage[:].bitcast(BF16)
        return v[:, i * 2048:(i + 1) * 2048]

    def outFM(mc):
        t = outFM_lo if mc < 8 else outFM_hi
        m = mc % 8
        return t[:][:, m * 512:(m + 1) * 512]

    ps_t = nc.alloc_psum_tensor("ps", [128, 4096], F32)
    ps = ps_t.ap()
    psb = ps.bitcast(BF16)

    def bank(b, lo=0, hi=512, parts=128):
        return ps[0:parts, b * 512 + lo: b * 512 + hi]

    def bankb(b, lo=0, hi=1024, parts=128):
        return psb[0:parts, b * 1024 + lo: b * 1024 + hi]

    sLD = Sem(nc, "sLD")
    sGP = Sem(nc, "sGP")
    sPC = Sem(nc, "sPC")
    sIO = Sem(nc, "sIO")
    sV = Sem(nc, "sV")
    sS = Sem(nc, "sS")
    sT = Sem(nc, "sT")
    sG = Sem(nc, "sG")

    ops = {"sync": [], "vector": [], "scalar": [], "tensor": [], "gpsimd": []}

    def emit(engn, maker, waits=(), inc=None, amt=1):
        wl = tuple((s.h, v) for (s, v) in waits if v > 0)
        if inc is not None:
            inc.v += amt
        ih = None if inc is None else inc.h

        def f(e, wl=wl, maker=maker, ih=ih, amt=amt):
            for (h, v) in wl:
                e.wait_ge(h, v)
            ins = maker(e)
            if ih is not None:
                ins.then_inc(ih, amt)
        ops[engn].append(f)
        return None if inc is None else inc.v

    def dma(engn, out, in_, waits=(), sem=None):
        sem = sem if sem is not None else (sGP if engn == "gpsimd" else sLD)
        return emit(engn, lambda e, o=out, i=in_: e.dma_start(out=o, in_=i),
                    waits, inc=sem, amt=16)

    cp = {}

    CONST_VALS = [0.0, 1.0, -1.0, -0.5, EPS, RN]
    for ci, val in enumerate(CONST_VALS):
        emit("gpsimd", lambda e, ci=ci, val=val: e.memset(consts[:, ci:ci + 1], val))
        nc.const_aps.register(F32, val, consts[:, ci:ci + 1])
    emit("gpsimd", lambda e: e.memset(identf, 0.0))
    emit("gpsimd", lambda e: e.affine_select(
        out=identf, in_=identf, compare_op=Alu.not_equal, fill=1.0, base=0,
        pattern=[[-1, 128]], channel_multiplier=1))
    emit("gpsimd", lambda e: e.memset(identb, 0.0))
    emit("gpsimd", lambda e: e.affine_select(
        out=identb, in_=identb, compare_op=Alu.not_equal, fill=1.0, base=0,
        pattern=[[-1, 128]], channel_multiplier=1))
    emit("gpsimd", lambda e: e.memset(carry, 0.0))
    emit("gpsimd", lambda e: e.memset(abuf, 0.0))
    emit("gpsimd", lambda e: e.memset(flagb, 1.0), inc=sG)
    cp["gp_pro"] = sG.v

    cp["alog_ld"] = dma("sync", A_raw, alog_h)
    cp["dvec_ld"] = dma("sync", D_sb, dvec_h)
    v1 = dma("sync", wf32[:, 0:128], wg_h)
    v2 = dma("sync", wf32[:, 128:640].rearrange("p (a b) -> p a b", a=16),
             wbc_h.transpose([1, 0, 2]))
    cp["wg_ld"] = emit("gpsimd", lambda e: e.tensor_copy(out=wg_bf, in_=wf32[:, 0:128]),
                       waits=[(sLD, v1)], inc=sG)
    cp["wbc_ld"] = emit("gpsimd", lambda e: e.tensor_copy(out=wxpbc_bf, in_=wf32[:, 128:640]),
                        waits=[(sLD, v2)], inc=sG)

    cp["A_exp"] = emit("scalar", lambda e: e.activation(out=A_sb, in_=A_raw, func=Act.Exp),
                       waits=[(sLD, cp["alog_ld"])], inc=sS)
    cp["A_neg"] = emit("vector", lambda e: e.tensor_scalar_mul(out=A_sb, in0=A_sb, scalar1=-1.0),
                       waits=[(sS, cp["A_exp"])], inc=sV)

    cp["wld_pro"] = sG.v

    xbufs = [xbuf0, xbuf1]

    for c in range(NCH):
        xb = xbufs[c % 2][:]
        w_x = []
        if c >= 2:
            w_x = [(sV, cp[("xdoneV", c - 2)]), (sS, cp[("xdoneS", c - 2)]),
                   (sLD, cp[("xpst", c - 2, 1)])]
        for ts in range(2):
            g = 2 * c + ts
            cp[("xld", c, ts)] = dma(
                "sync", xb[:, ts * DM:(ts + 1) * DM], x_h[g * 128:(g + 1) * 128, :],
                waits=w_x)

        w_h = [(sV, cp[("scanf", c - 1)])] if c >= 1 else []
        for ts in range(2):
            xt = xb[:, ts * DM:(ts + 1) * DM]
            dummy = tmbuf[:][:, ts * DM:(ts + 1) * DM]
            sc = 16 * ts
            emit("scalar", lambda e, xt=xt, d=dummy, o=stats[:, sc:sc + 1]:
                 e.activation(out=d, in_=xt, func=Act.Copy, accum_out=o),
                 waits=[(sLD, cp[("xld", c, ts)])] + w_h, inc=sS)
            cp[("sum", c, ts)] = emit(
                "scalar", lambda e, xt=xt, d=dummy, o=stats[:, sc + 1:sc + 2]:
                e.activation(out=d, in_=xt, func=Act.Square, accum_out=o), inc=sS)

        def stat_chain(base, sumkey, invkey, nmikey):
            for ts in range(2):
                sc = base + 16 * ts
                emit("vector", lambda e, sc=sc: e.tensor_scalar(
                    out=stats[:, sc + 2:sc + 3], in0=stats[:, sc:sc + 1],
                    scalar1=RN, scalar2=None, op0=Alu.mult),
                    waits=[(sS, cp[(sumkey, c, ts)])], inc=sV)
                emit("vector", lambda e, sc=sc: e.tensor_scalar(
                    out=stats[:, sc + 3:sc + 4], in0=stats[:, sc + 1:sc + 2],
                    scalar1=RN, scalar2=None, op0=Alu.mult), inc=sV)
            vA = sV.v
            for ts in range(2):
                sc = base + 16 * ts
                emit("vector", lambda e, sc=sc: e.tensor_mul(
                    out=stats[:, sc + 4:sc + 5], in0=stats[:, sc + 2:sc + 3],
                    in1=stats[:, sc + 2:sc + 3]),
                    waits=[(sV, vA)] if ts == 0 else (), inc=sV)
            vB = sV.v
            for ts in range(2):
                sc = base + 16 * ts
                emit("vector", lambda e, sc=sc: e.tensor_sub(
                    out=stats[:, sc + 5:sc + 6], in0=stats[:, sc + 3:sc + 4],
                    in1=stats[:, sc + 4:sc + 5]),
                    waits=[(sV, vB)] if ts == 0 else (), inc=sV)
            vC = sV.v
            for ts in range(2):
                sc = base + 16 * ts
                emit("scalar", lambda e, sc=sc: e.activation(
                    out=stats[:, sc + 6:sc + 7], in_=stats[:, sc + 5:sc + 6],
                    func=Act.Ln, bias=EPS),
                    waits=([(sV, vC), (sG, cp["gp_pro"])] if ts == 0 else ()), inc=sS)
            for ts in range(2):
                sc = base + 16 * ts
                cp[(invkey, c, ts)] = emit("scalar", lambda e, sc=sc: e.activation(
                    out=stats[:, sc + 7:sc + 8], in_=stats[:, sc + 6:sc + 7],
                    func=Act.Exp, scale=-0.5), inc=sS)
            for ts in range(2):
                sc = base + 16 * ts
                cp[(nmikey, c, ts)] = emit("vector", lambda e, sc=sc: e.scalar_tensor_tensor(
                    out=stats[:, sc + 8:sc + 9], in0=stats[:, sc + 2:sc + 3],
                    scalar=-1.0, in1=stats[:, sc + 7:sc + 8],
                    op0=Alu.mult, op1=Alu.mult),
                    waits=[(sS, cp[(invkey, c, 1)])] if ts == 0 else (), inc=sV)

        stat_chain(0, "sum", "inv", "nmi")
        for ts in range(2):
            sc = 16 * ts
            xt = xb[:, ts * DM:(ts + 1) * DM]
            wts = [(sV, cp[("nmi", c, ts)])]
            cp[("ln1tm", c, ts)] = emit(
                "scalar", lambda e, xt=xt, sc=sc, o=tmbuf[:][:, ts * DM:(ts + 1) * DM]:
                e.activation(out=o, in_=xt, func=Act.Identity,
                             scale=stats[:, sc + 7:sc + 8], bias=stats[:, sc + 8:sc + 9]),
                waits=wts, inc=sS)

        for grp in range(8):
            ts, jq = grp // 4, grp % 4
            bk = 6 + grp % 2
            for k in range(4):
                j = jq * 4 + k
                src = tmbuf[:][:, ts * DM + j * 128: ts * DM + (j + 1) * 128]
                wts = []
                if k == 0:
                    wts.append((sS, cp[("ln1tm", c, ts)]))
                    if grp >= 2:
                        wts.append((sS, cp[("trev", c, grp - 2)]))
                    elif c >= 1:
                        wts.append((sS, cp[("spfree", c - 1, 4 + grp % 2)]))
                        wts.append((sS, cp[("spfree", c - 1, 6 + grp % 2)]))
                    if grp == 0:
                        wts.append((sG, cp["gp_pro"]))
                emit("tensor", lambda e, s=src, d=bank(bk, k * 128, (k + 1) * 128):
                     e.transpose(d, s, identf), waits=wts, inc=sT)
            vtr = sT.v
            o32 = fmbuf[:].rearrange("p (a b) -> p a b", a=NJ)[:, jq * 4:(jq + 1) * 4,
                                                              ts * 128:(ts + 1) * 128]
            ob = ln1fb[:].rearrange("p (a b) -> p a b", a=NJ)[:, jq * 4:(jq + 1) * 4,
                                                              ts * 128:(ts + 1) * 128]
            wev = [(sT, vtr)]
            if grp == 0 and c >= 1:
                wev.append((sLD, cp[("l2st", c - 1)]))
            emit("scalar", lambda e, o=o32, b2=bank(bk): e.copy(out=o, in_=b2),
                 waits=wev, inc=sS)
            cp[("trev", c, grp)] = emit(
                "scalar", lambda e, o=ob, b2=bank(bk): e.copy(out=o, in_=b2), inc=sS)
        cp[("ln1fm", c)] = sS.v

        wbc0 = [(sS, cp[("ln1fm", c)]), (sG, cp["wbc_ld"])]
        if c >= 1:
            wbc0.append((sV, cp[("ytrev", c - 1)]))
        for k in range(16):
            emit("tensor", lambda e, k=k: e.matmul(
                bank(2, 0, 256, parts=32),
                wxpbc_bf[:, k * 32:(k + 1) * 32],
                ln1fb[:][:, k * TS:(k + 1) * TS],
                start=(k == 0), stop=(k == 15)),
                waits=(wbc0 if k == 0 else ()), inc=sT)
        cp[("bcmm", c)] = sT.v
        cp[("bcev", c)] = emit("scalar",
                               lambda e: e.copy(out=bc_sb, in_=bank(2, 0, 256, parts=32)),
                               waits=[(sT, cp[("bcmm", c)])], inc=sS)

        def wxp_load(k):
            wts = []
            if (c, k) == (0, 0):
                wts.append((sG, cp["wbc_ld"]))
            kk = ("wxc", c, k - 2) if k >= 2 else (("wxc", c - 1, 14 + k) if c >= 1 else None)
            if kk is not None:
                wts.append((sG, cp[kk]))
            cp[("wxl", c, k)] = dma(
                "sync", wf32[:, (k % 2) * 2048:((k % 2) + 1) * 2048], wxp_h[k], waits=wts)

        def wxp_cast(k):
            wts = [(sLD, cp[("wxl", c, k)])]
            kk = ("xpK", c, k - 2) if k >= 2 else (("xpK", c - 1, 14 + k) if c >= 1 else None)
            if kk is not None:
                wts.append((sT, cp[kk]))
            cp[("wxc", c, k)] = emit("gpsimd", lambda e, k=k: e.tensor_copy(
                out=wxpK[:, (k % 2) * 2048:((k % 2) + 1) * 2048],
                in_=wf32[:, (k % 2) * 2048:((k % 2) + 1) * 2048]),
                waits=wts, inc=sG)

        wxp_load(0)
        wxp_load(1)
        wxp_cast(0)
        wxp_cast(1)
        for k in range(16):
            for mc in range(16):
                wts = []
                if mc == 0:
                    wts.append((sG, cp[("wxc", c, k)]))
                    if k == 0:
                        wts.append((sS, cp[("bcev", c)]))
                        if c >= 1:
                            wts.append((sS, cp[("spfree_all", c - 1)]))
                            wts.append((sV, cp[("msk", c - 1, 1)]))
                emit("tensor", lambda e, k=k, mc=mc: e.matmul(
                    ps[:, mc * 256:(mc + 1) * 256],
                    wxpK[:, (k % 2) * 2048 + mc * 128:(k % 2) * 2048 + (mc + 1) * 128],
                    ln1fb[:][:, k * TS:(k + 1) * TS],
                    start=(k == 0 and mc % 2 == 0), stop=(k == 15)),
                    waits=wts, inc=sT)
            cp[("xpK", c, k)] = sT.v
            if k + 2 < 16:
                wxp_load(k + 2)
                wxp_cast(k + 2)
        cp[("xproj", c)] = sT.v

        SLOT = lambda q, i: hbuf[:][:, q * 2048 + i * 256: q * 2048 + (i + 1) * 256]
        for p in range(8):
            q = p % 2
            mcs = (2 * p, 2 * p + 1)
            for i, mc in enumerate(mcs):
                wts = []
                if p == 0 and i == 0:
                    wts = [(sT, cp[("xproj", c)])]
                    if c >= 1:
                        wts.append((sV, cp[("w", c - 1, 15)]))
                if p >= 2 and i == 0:
                    wts.append((sV, cp[("dtadd", c, 2 * (p - 2) + 1)]))
                emit("scalar", lambda e, mc=mc, o=SLOT(q, 0 + i): e.activation(
                    out=o, in_=ps[:, mc * 256:(mc + 1) * 256], func=Act.Abs),
                    waits=wts, inc=sS)
            for i, mc in enumerate(mcs):
                emit("scalar", lambda e, mc=mc, o=SLOT(q, 2 + i): e.activation(
                    out=o, in_=ps[:, mc * 256:(mc + 1) * 256], func=Act.Relu), inc=sS)
            cp[("spfree", c, p)] = sS.v
            for i in range(2):
                emit("scalar", lambda e, i=i, q=q: e.activation(
                    out=SLOT(q, 4 + i), in_=SLOT(q, 0 + i), func=Act.Exp, scale=-1.0), inc=sS)
            for i in range(2):
                emit("scalar", lambda e, i=i, q=q: e.activation(
                    out=SLOT(q, 6 + i), in_=SLOT(q, 4 + i), func=Act.Ln, bias=1.0), inc=sS)
            vsp = sS.v
            for i, mc in enumerate(mcs):
                cp[("dtadd", c, mc)] = emit(
                    "vector", lambda e, mc=mc, i=i, q=q: e.tensor_add(
                        out=dtb[:][:, mc * 256:(mc + 1) * 256],
                        in0=SLOT(q, 6 + i), in1=SLOT(q, 2 + i)),
                    waits=[(sS, vsp)], inc=sV)
        cp[("spfree_all", c)] = sS.v
        cp[("dt", c)] = sV.v
        cp[("dtx", c)] = emit("vector", lambda e: e.tensor_mul(
            out=dtxb, in0=dtb, in1=fmbuf),
            waits=[(sV, cp[("dt", c)])], inc=sV)
        for j in range(NJ):
            wts = []
            if j == 0:
                wts = [(sV, cp[("dtx", c)]), (sLD, cp["dvec_ld"])]
                if c >= 1:
                    wts.append((sV, cp[("ytrev", c - 1)]))
                    wts.append((sG, cp[("ygf", c - 1)]))
            cp[("yini", c, j)] = emit("scalar", lambda e, j=j: e.activation(
                out=ybuf[:][:, j * TS:(j + 1) * TS],
                in_=fmbuf[:][:, j * TS:(j + 1) * TS],
                func=Act.Identity, scale=D_sb[:, j:j + 1]),
                waits=wts, inc=sS)
        cp[("yini", c)] = sS.v

        SW = TS + 1
        a_bufs = [abuf[:], tmbuf[:][:, 0:NJ * SW]]
        a3 = [b.rearrange("p (a b) -> p a b", a=NJ) for b in a_bufs]
        u3 = ubuf[:][:, 0:NJ * SW].rearrange("p (a b) -> p a b", a=NJ)
        h3 = hbuf[:][:, 0:NJ * SW].rearrange("p (a b) -> p a b", a=NJ)
        d3 = lambda ap: ap.rearrange("p (a b) -> p a b", a=NJ)
        ublk = u3[:, :, 1:SW]
        hblk = h3[:, :, 1:SW]

        cp[("asep", c)] = emit("gpsimd", lambda e: e.memset(a3[1][:, :, 0:1], 0.0),
                               waits=[(sT, cp[("bcmm", c)])] +
                                     ([(sV, cp[("scans", c - 1, 15)])] if c >= 1 else []),
                               inc=sG)

        def selmm(n):
            par = n % 2
            wts = [(sS, cp[("bcev", c)]), (sS, cp[("spfree", c, par)])]
            if n >= 2:
                wts.append((sS, cp[("bcbev", c, n - 2)]))
            emit("tensor", lambda e, n=n, par=par: e.matmul(
                bank(par, 0, 256),
                identf[0:32, n:n + 1].broadcast_to([32, 128]), bc_sb,
                start=True, stop=False), waits=wts, inc=sT)
            cp[("selmm", c, n)] = emit("tensor", lambda e, n=n, par=par: e.matmul(
                bank(par, 256, 512),
                identf[0:32, (n + 16):(n + 17)].broadcast_to([32, 128]), bc_sb,
                start=False, stop=True), inc=sT)

        def bcbevac(n):
            par = n % 2
            bb = bb_sb[:, par * 256:(par + 1) * 256]
            cb = cb_sb[:, par * 256:(par + 1) * 256]
            wev = [(sT, cp[("selmm", c, n)])]
            if n >= 2:
                wev.append((sV, cp[("u", c, n - 2)]))
                wev.append((sV, cp[("w", c, n - 2)]))
            emit("scalar", lambda e, bb=bb, par=par: e.copy(out=bb, in_=bank(par, 0, 256)),
                 waits=wev, inc=sS)
            cp[("bcbev", c, n)] = emit(
                "scalar", lambda e, cb=cb, par=par: e.copy(out=cb, in_=bank(par, 256, 512)),
                inc=sS)

        def agen(n):
            par = n % 2
            ab = a_bufs[par]
            for j in range(NJ):
                wts = []
                if j == 0:
                    wts = [(sV, cp[("dt", c)])]
                    if n >= 2:
                        wts.append((sV, cp[("scans", c, n - 2)]))
                    elif c >= 1:
                        wts.append((sV, cp[("scans", c - 1, 14 + n)]))
                    if par == 1:
                        wts.append((sG, cp[("asep", c)]))
                cp[("a", c, n, j)] = emit("scalar", lambda e, j=j, n=n, ab=ab: e.activation(
                    out=ab[:, j * SW + 1:(j + 1) * SW],
                    in_=dtb[:][:, j * TS:(j + 1) * TS],
                    func=Act.Exp, scale=A_sb[:, (j * NST + n):(j * NST + n + 1)]),
                    waits=wts, inc=sS)
            cp[("a", c, n)] = sS.v

        def v_steps(n):
            par = n % 2
            bb = bb_sb[:, par * 256:(par + 1) * 256]
            cb = cb_sb[:, par * 256:(par + 1) * 256]
            ab = a_bufs[par]
            wts = [(sS, cp[("bcbev", c, n)])]
            if n == 0:
                wts.append((sV, cp[("dtx", c)]))
            if n >= 1:
                wts.append((sG, cp[("yg", c, n - 1)]))
            elif c >= 1:
                wts.append((sG, cp[("yg", c - 1, 15)]))
            emit("vector", lambda e, bb=bb: e.tensor_tensor(
                out=ublk, in0=d3(dtxb[:]),
                in1=bb.unsqueeze(1).broadcast_to([128, NJ, TS]), op=Alu.mult),
                waits=wts, inc=sV)
            emit("vector", lambda e, ab=ab: e.memset(
                ab.rearrange("p (a b) -> p a b", a=NJ)[:, :, 0:1], 0.0), inc=sV)
            rpar = ((c + 1) % 2) * 256
            cp[("u", c, n)] = emit("vector", lambda e, n=n, rpar=rpar: e.tensor_copy(
                out=u3[:, :, 0:1],
                in_=carry[:, rpar + n:rpar + 256:NST].unsqueeze(2)), inc=sV)
            wts = [(sS, cp[("a", c, n)]), (sV, cp[("u", c, n)])]
            if c == 0 and n == 0:
                wts.append((sG, cp["gp_pro"]))
            cp[("scan", c, n)] = emit(
                "vector", lambda e, ab=ab: e.tensor_tensor_scan(
                    out=hbuf[:][:, 0:NJ * SW], data0=ab[:, 0:NJ * SW],
                    data1=ubuf[:][:, 0:NJ * SW], initial=0.0,
                    op0=Alu.mult, op1=Alu.add),
                waits=wts, inc=sV)
            cp[("scans", c, n)] = sV.v
            wpar = (c % 2) * 256
            cp[("cry", c, n)] = emit("vector", lambda e, n=n, wpar=wpar: e.tensor_copy(
                out=carry[:, wpar + n:wpar + 256:NST].unsqueeze(2),
                in_=h3[:, :, 256:257]),
                waits=[(sV, cp[("scans", c, n)])], inc=sV)
            cp[("w", c, n)] = emit("vector", lambda e, cb=cb: e.tensor_tensor(
                out=ublk, in0=hblk,
                in1=cb.unsqueeze(1).broadcast_to([128, NJ, TS]), op=Alu.mult),
                waits=[(sV, cp[("scans", c, n)])], inc=sV)
            wts = [(sV, cp[("w", c, n)])]
            if n == 0:
                wts.append((sS, cp[("yini", c)]))
            cp[("yg", c, n)] = emit("gpsimd", lambda e: e.tensor_tensor(
                out=d3(ybuf[:]), in0=d3(ybuf[:]), in1=ublk, op=Alu.add),
                waits=wts, inc=sG)

        selmm(0)
        agen(0)
        bcbevac(0)
        selmm(1)
        agen(1)
        for n in range(NST):
            if n + 2 < NST:
                selmm(n + 2)
            v_steps(n)
            if n + 1 < NST:
                bcbevac(n + 1)
            if n + 2 < NST:
                agen(n + 2)
        cp[("scan_done", c)] = cp[("scans", c, 15)]
        cp[("scanf", c)] = cp[("scans", c, 15)]
        cp[("ygf", c)] = cp[("yg", c, 15)]

        for grp in range(8):
            ts, jq = grp // 4, grp % 4
            bk = 2 + grp % 2
            for k in range(4):
                j = jq * 4 + k
                src = ybuf[:][:, j * TS + ts * 128: j * TS + (ts + 1) * 128]
                wts = []
                if k == 0:
                    wts.append((sG, cp[("ygf", c)]))
                    wts.append((sS, cp[("spfree_all", c)]))
                    if grp >= 2:
                        wts.append((sV, cp[("ytr", c, grp - 2)]))
                emit("tensor", lambda e, s=src, d=bank(bk, k * 128, (k + 1) * 128):
                     e.transpose(d, s, identf), waits=wts, inc=sT)
            vtr = sT.v
            xs = xb[:, ts * DM + jq * 512: ts * DM + (jq + 1) * 512]
            cp[("ytr", c, grp)] = emit("vector", lambda e, b2=bank(bk), xs=xs:
                                       e.tensor_add(out=xs, in0=b2, in1=xs),
                                       waits=[(sT, vtr)], inc=sV)
        cp[("ytrev", c)] = sV.v
        cp[("xdoneV", c)] = sV.v
        for ts in range(2):
            cp[("xpst", c, ts)] = dma("sync", xp_s[(2 * c + ts) * 128:(2 * c + ts + 1) * 128, :],
                                      xb[:, ts * DM:(ts + 1) * DM],
                                      waits=[(sV, cp[("ytrev", c)])])
        ln2tm = fmbuf[:].bitcast(BF16)[:, 0:2 * DM]
        for ts in range(2):
            xpt = xb[:, ts * DM:(ts + 1) * DM]
            dummy = ln1fb[:][:, ts * DM:(ts + 1) * DM]
            sc = 32 + 16 * ts
            emit("scalar", lambda e, xt=xpt, d=dummy, o=stats[:, sc:sc + 1]:
                 e.activation(out=d, in_=xt, func=Act.Copy, accum_out=o),
                 waits=[(sV, cp[("ytrev", c)])] if ts == 0 else (), inc=sS)
            cp[("sum2", c, ts)] = emit(
                "scalar", lambda e, xt=xpt, d=dummy, o=stats[:, sc + 1:sc + 2]:
                e.activation(out=d, in_=xt, func=Act.Square, accum_out=o), inc=sS)
        stat_chain(32, "sum2", "inv2", "nmi2")
        for ts in range(2):
            sc = 32 + 16 * ts
            xpt = xb[:, ts * DM:(ts + 1) * DM]
            cp[("ln2tm", c, ts)] = emit(
                "scalar", lambda e, xt=xpt, sc=sc, o=ln2tm[:, ts * DM:(ts + 1) * DM]:
                e.activation(out=o, in_=xt, func=Act.Identity,
                             scale=stats[:, sc + 7:sc + 8], bias=stats[:, sc + 8:sc + 9]),
                waits=[(sV, cp[("nmi2", c, ts)])], inc=sS)
        ln2fm = fmbuf[:].bitcast(BF16)[:, 2 * DM:2 * DM + NJ * TS]
        for grp in range(4):
            ts, jh = grp // 2, grp % 2
            bk = 2 + grp % 2
            for k in range(8):
                j = jh * 8 + k
                src = ln2tm[:, ts * DM + j * 128: ts * DM + (j + 1) * 128]
                wts = []
                if k == 0:
                    wts.append((sS, cp[("ln2tm", c, ts)]))
                    wts.append((sV, cp[("ytrev", c)]))
                    if grp >= 2:
                        wts.append((sS, cp[("l2ev", c, grp - 2)]))
                emit("tensor", lambda e, s=src, d=bankb(bk, k * 128, (k + 1) * 128):
                     e.transpose(d, s, identb), waits=wts, inc=sT)
            vtr = sT.v
            ofm = ln2fm.rearrange("p (a b) -> p a b", a=NJ)[:, jh * 8:(jh + 1) * 8,
                                                            ts * 128:(ts + 1) * 128]
            cp[("l2ev", c, grp)] = emit(
                "scalar", lambda e, o=ofm, b2=bankb(bk): e.copy(out=o, in_=b2),
                waits=[(sT, vtr)], inc=sS)
        cp[("ln2fm", c)] = sS.v
        cp[("l2st", c)] = dma("sync", ln2_s[c], ln2fm, waits=[(sS, cp[("ln2fm", c)])])
        cp[("lntm_free", c)] = sS.v

        for k in range(16):
            wts = []
            if k == 0:
                wts = [(sS, cp[("ln2fm", c)]), (sG, cp["wg_ld"])]
                if c >= 1:
                    wts.append((sS, cp[("lgev", c - 1)]))
            emit("tensor", lambda e, k=k: e.matmul(
                bank(2, 0, 256, parts=8), wg_bf[:, k * 8:(k + 1) * 8],
                ln2fm[:, k * TS:(k + 1) * TS],
                start=(k == 0), stop=(k == 15)), waits=wts, inc=sT)
        vlg = sT.v
        cp[("lgev", c)] = emit("scalar",
                               lambda e: e.copy(out=lg_sb, in_=bank(2, 0, 256, parts=8)),
                               waits=[(sT, vlg)], inc=sS)
        for ts in range(2):
            g = 2 * c + ts
            lgT = bank(3, ts * 8, (ts + 1) * 8)
            wts = [(sS, cp[("lgev", c)])]
            if ts == 1:
                wts.append((sS, cp[("vex", c, 0)]))
                wts.append((sV, cp[("msk", c, 0)]))
            elif c >= 1:
                wts.append((sS, cp[("vex", c - 1, 1)]))
                wts.append((sV, cp[("msk", c - 1, 1)]))
            emit("tensor", lambda e, s=lg_sb[:, ts * 128:(ts + 1) * 128], d=lgT:
                 e.transpose(d, s, identf[0:8, 0:8]), waits=wts, inc=sT)
            vt = sT.v
            nmcol = stats[:, 60:61]
            secol = stats[:, 61:62]
            e8 = stats[:, 48:56]
            vnm = emit("vector", lambda e, lgT=lgT, o=nmcol: e.reduce_max(
                out=o, in_=lgT, axis=AX, negate=True),
                waits=[(sT, vt)], inc=sV)
            cp[("vex", c, ts)] = emit(
                "scalar", lambda e, lgT=lgT, o=e8, nm=nmcol, se=secol: e.activation(
                    out=o, in_=lgT, func=Act.Exp, bias=nm, accum_out=se),
                waits=[(sV, vnm)], inc=sS)
            emit("vector", lambda e, g=g, se=secol: e.reciprocal(
                out=gate[:, g:g + 1], in_=se), waits=[(sS, cp[("vex", c, ts)])], inc=sV)
            cp[("msk", c, ts)] = emit("vector", lambda e, g=g, lgT=lgT, nm=nmcol:
                                      e.tensor_scalar(
                                          out=masks[:, g * 8:(g + 1) * 8], in0=lgT,
                                          scalar1=nm, scalar2=0.0,
                                          op0=Alu.add, op1=Alu.is_ge),
                                      waits=[(sV, vnm)], inc=sV)
        cp[("xdoneS", c)] = sS.v

    vprev = cp[("msk", NCH - 1, 1)]
    for e_ in range(NE):
        mslice = masks[:, e_:128:8]
        v1 = emit("vector", lambda e, m=mslice: e.tensor_mul(out=m, in0=m, in1=flagb),
                  waits=[(sV, vprev), (sG, cp["gp_pro"])], inc=sV)
        v2 = emit("vector", lambda e, m=mslice: e.tensor_scalar(
            out=tmp16, in0=m, scalar1=-1.0, scalar2=1.0, op0=Alu.mult, op1=Alu.add),
            waits=[(sV, v1)], inc=sV)
        vprev = emit("vector", lambda e: e.tensor_mul(out=flagb, in0=flagb, in1=tmp16),
                     waits=[(sV, v2)], inc=sV)
    cp["tiebreak"] = vprev
    for grp in range(4):
        bk = 6 + grp % 2
        for k in range(4):
            tsg = grp * 4 + k
            src = masks[:, tsg * 8:(tsg + 1) * 8]
            wts = []
            if k == 0:
                wts.append((sV, cp["tiebreak"]))
                wts.append((sS, cp[("vex", NCH - 1, 1)]))
                if grp >= 2:
                    wts.append((sS, cp[("mtev", grp - 2)]))
            emit("tensor", lambda e, s=src, d=bank(bk, k * 128, (k + 1) * 128, parts=8):
                 e.transpose(d, s, identf), waits=wts, inc=sT)
        vtr = sT.v
        cp[("mtev", grp)] = emit(
            "scalar", lambda e, grp=grp, bk=bk: e.copy(
                out=masksT[:, grp * 512:(grp + 1) * 512],
                in_=bank(bk, 0, 512, parts=8)),
            waits=[(sT, vtr)], inc=sS)
    cp["masksT"] = sS.v
    cp["A_end_V"] = sV.v
    cp["A_end_S"] = sS.v
    cp["A_end_T"] = sT.v
    cp["A_end_G"] = sG.v

    def _prev_unit(tb, eh, mc, e4, back):
        idx = ((tb * 2 + eh) * 16 + mc) * 4 + e4 - (2 if back == 2 else 4)
        if idx < 0:
            return None
        e4b = idx % 4
        mcb = (idx // 4) % 16
        ehb = (idx // 64) % 2
        tbb = idx // 128
        return (tbb, ehb, mcb, e4b)

    vb = dma("sync", wf32[0:8, 2048:4096], b_h,
             waits=[(sG, cp[("wxc", NCH - 1, 15)]), (sT, cp["A_end_T"]),
                    (sS, cp["A_end_S"])])
    cp["b_ld"] = emit("gpsimd", lambda e: e.tensor_copy(out=b_bf, in_=wf32[0:8, 2048:4096]),
                      waits=[(sLD, vb)], inc=sG)
    NKC = 16
    last_meb_ev = 0
    last_xe = 0
    for tb in range(NTBG):
        wts = [(sV, cp["A_end_V"]), (sS, cp["A_end_S"]), (sT, cp["A_end_T"]),
               (sG, cp["A_end_G"])]
        if tb >= 1:
            wts.append((sT, cp[("mmall", tb - 1)]))
        for hf in range(2):
            cp[("l2ld", tb, hf)] = dma(
                "sync", ln2TBG[:, hf * 4096:(hf + 1) * 4096], ln2_s[tb * 2 + hf],
                waits=(wts if hf == 0 else []))
        for eh in range(2):
            for e4 in range(4):
                ex = eh * 4 + e4
                wts = [(sS, cp["masksT"])]
                if last_meb_ev:
                    wts.append((sS, last_meb_ev))
                vmm = emit("tensor", lambda e, ex=ex, tb=tb: e.matmul(
                    bank(2), identb[0:8, ex:ex + 1].broadcast_to([8, 128]),
                    masksT[:, tb * 512:(tb + 1) * 512],
                    start=True, stop=True), waits=wts, inc=sT)
                wev = [(sT, vmm)]
                if last_xe:
                    wev.append((sV, last_xe))
                last_meb_ev = emit("scalar", lambda e: e.copy(out=meb_bf, in_=bank(2)),
                                   waits=wev, inc=sS)
                wxe = [(sS, last_meb_ev), (sLD, cp[("l2ld", tb, 1)])]
                if tb > 0 or eh > 0:
                    prev = ("mmeh", tb, eh - 1) if eh > 0 else ("mmeh", tb - 1, 1)
                    wxe.append((sT, cp[prev]))
                src0 = ln2TBG.rearrange("p (b a c) -> p a b c", b=2, a=NJ)
                src1 = meb_bf.rearrange("p (b c) -> p b c", b=2).unsqueeze(1)\
                    .broadcast_to([128, NJ, 2, 256])
                dst = xe[e4].rearrange("p (a b c) -> p a b c", a=NJ, b=2)
                last_xe = emit(
                    "vector", lambda e, d=dst, s0=src0, s1=src1: e.tensor_tensor(
                        out=d, in0=s0, in1=s1, op=Alu.mult),
                    waits=wxe, inc=sV)
                cp[("xe", tb, eh, e4)] = last_xe
            for mc in range(16):
                for e4 in range(4):
                    ex = eh * 4 + e4
                    si = (mc * 4 + e4) % 4
                    wts = []
                    key = ("wcast", tb, eh, mc, e4)
                    prev2 = _prev_unit(tb, eh, mc, e4, 2)
                    if prev2 is not None:
                        wts.append((sG, cp[("wcast",) + prev2]))
                    else:
                        wts.append((sG, cp[("wxc", NCH - 1, 15)]))
                        wts.append((sG, cp["b_ld"]))
                    cp[("wf32", tb, eh, mc, e4)] = dma(
                        "sync", wf32[:, (si % 2) * 2048:((si % 2) + 1) * 2048],
                        w5_h[mc, ex], waits=wts)
                    wts = [(sLD, cp[("wf32", tb, eh, mc, e4)])]
                    prev4 = _prev_unit(tb, eh, mc, e4, 1)
                    key4 = ("wsl",) + prev4 if prev4 is not None else None
                    if key4 is not None and key4 in cp:
                        wts.append((sT, cp[key4]))
                    cp[("wcast", tb, eh, mc, e4)] = emit(
                        "gpsimd", lambda e, si=si: e.tensor_copy(
                            out=wslot(si),
                            in_=wf32[:, (si % 2) * 2048:((si % 2) + 1) * 2048]),
                        waits=wts, inc=sG)
                bk = mc % 2
                if eh == 0:
                    wts = [(sS, cp["masksT"]), (sG, cp["b_ld"])]
                    if mc >= 2:
                        kf = cp[("evB", tb, 0, mc - 2)]
                        wts.append((kf[0], kf[1]))
                    elif tb >= 1:
                        kf = cp[("evB", tb - 1, 1, 14 + mc)]
                        wts.append((kf[0], kf[1]))
                    emit("tensor", lambda e, mc=mc, tb=tb, bk=bk: e.matmul(
                        bank(bk), b_bf[:, mc * 128:(mc + 1) * 128],
                        masksT[:, tb * 512:(tb + 1) * 512],
                        start=True, stop=False), waits=wts, inc=sT)
                for e4 in range(4):
                    si = (mc * 4 + e4) % 4
                    for k in range(NKC):
                        wts = []
                        if k == 0:
                            wts.append((sG, cp[("wcast", tb, eh, mc, e4)]))
                            wts.append((sV, cp[("xe", tb, eh, e4)]))
                            if e4 == 0 and eh == 1:
                                if mc >= 2:
                                    kf = cp[("evB", tb, 1, mc - 2)]
                                else:
                                    kf = cp[("evB", tb, 0, 14 + mc)]
                                wts.append((kf[0], kf[1]))
                        emit("tensor", lambda e, si=si, k=k, e4=e4, bk=bk, eh=eh:
                             e.matmul(
                                 bank(bk),
                                 wslot(si)[:, k * 128:(k + 1) * 128],
                                 xe[e4][:, k * 512:(k + 1) * 512],
                                 start=(eh == 1 and e4 == 0 and k == 0),
                                 stop=(e4 == 3 and k == NKC - 1)),
                             waits=wts, inc=sT)
                    cp[("wsl", tb, eh, mc, e4)] = sT.v
                vmm = sT.v
                if eh == 0:
                    cp[("evB", tb, 0, mc)] = (sS, emit(
                        "scalar", lambda e, mc=mc, bk=bk: e.copy(out=outFM(mc), in_=bank(bk)),
                        waits=[(sT, vmm)], inc=sS))
                else:
                    kf = cp[("evB", tb, 0, mc)]
                    cp[("evB", tb, 1, mc)] = (sV, emit(
                        "vector", lambda e, mc=mc, bk=bk: e.tensor_add(
                            out=outFM(mc), in0=outFM(mc), in1=bank(bk)),
                        waits=[(sT, vmm), (kf[0], kf[1])], inc=sV))
            cp[("mmeh", tb, eh)] = sT.v
        cp[("mmall", tb)] = sT.v
        for ts in range(4):
            gidx = tb * 4 + ts
            wxp = [(sLD, cp[("xpst", gidx // 2, gidx % 2)])]
            if tb == 0 and ts < 2:
                wxp += [(sV, cp["A_end_V"]), (sS, cp["A_end_S"])]
            if ts >= 2:
                wxp.append((sGP, cp[("outst", tb, ts - 2)]))
            elif tb >= 1:
                wxp.append((sGP, cp[("outst", tb - 1, ts + 2)]))
            cp[("xpl", tb, ts)] = dma(
                "gpsimd", xptile[ts % 2], xp_s[gidx * 128:(gidx + 1) * 128, :], waits=wxp)
            for grp in range(4):
                bk = 3 + grp % 2
                for k in range(4):
                    mc = grp * 4 + k
                    src = outFM(mc)[:, ts * 128:(ts + 1) * 128]
                    wts = []
                    if k == 0:
                        kf = cp[("evB", tb, 1, grp * 4 + 3)]
                        wts.append((kf[0], kf[1]))
                        if grp >= 2:
                            wts.append((sV, cp[("otr", tb, ts, grp - 2)]))
                        elif ts >= 1:
                            wts.append((sV, cp[("otr", tb, ts - 1, grp + 2)]))
                        elif tb >= 1:
                            wts.append((sV, cp[("otr", tb - 1, 3, grp + 2)]))
                    emit("tensor", lambda e, s=src, d=bank(bk, k * 128, (k + 1) * 128):
                         e.transpose(d, s, identf), waits=wts, inc=sT)
                vtr = sT.v
                vsc = emit("scalar", lambda e, gidx=gidx, bk=bk, o=otmp[grp % 2]:
                           e.activation(out=o, in_=bank(bk), func=Act.Identity,
                                        scale=gate[:, gidx:gidx + 1]),
                           waits=[(sT, vtr)], inc=sS)
                wadd = [(sS, vsc), (sGP, cp[("xpl", tb, ts)])]
                if grp == 0:
                    if ts >= 2:
                        wadd.append((sGP, cp[("outst", tb, ts - 2)]))
                    elif tb >= 1:
                        wadd.append((sGP, cp[("outst", tb - 1, ts + 2)]))
                cp[("otr", tb, ts, grp)] = emit(
                    "vector", lambda e, ts=ts, grp=grp, o=otmp[grp % 2]: e.tensor_add(
                        out=outtile[ts % 2][:, grp * 512:(grp + 1) * 512],
                        in0=o, in1=xptile[ts % 2][:, grp * 512:(grp + 1) * 512]),
                    waits=wadd, inc=sV)
            cp[("outst", tb, ts)] = dma(
                "gpsimd", y_h[gidx * 128:(gidx + 1) * 128, :], outtile[ts % 2],
                waits=[(sV, cp[("otr", tb, ts, 3)])])

    final_gp = sGP.v
    emit("gpsimd", lambda e: e.engine_nop(), waits=[(sGP, final_gp)])

    with nc.Block() as block:
        @block.sync
        def _(sync):
            for f in ops["sync"]:
                f(sync)

        @block.vector
        def _(vector):
            for f in ops["vector"]:
                f(vector)

        @block.scalar
        def _(scalar):
            for f in ops["scalar"]:
                f(scalar)

        @block.tensor
        def _(tensor):
            for f in ops["tensor"]:
                f(tensor)

        @block.gpsimd
        def _(gpsimd):
            for f in ops["gpsimd"]:
                f(gpsimd)

    return nc


def prep_inputs(L0_attn_x, L0_attn_A_log, L0_attn_D, L0_attn_W_xproj,
                L0_ffn_wg, L0_ffn_W, L0_ffn_b):
    wxp = np.ascontiguousarray(
        np.asarray(L0_attn_W_xproj[:2048], np.float32).T.reshape(16, 128, 2048))
    wbc = np.ascontiguousarray(
        np.asarray(L0_attn_W_xproj[2048:2080], np.float32).T.reshape(16, 128, 32))
    alog = np.ascontiguousarray(
        np.asarray(L0_attn_A_log, np.float32).reshape(16, 128, 16)
        .transpose(1, 0, 2).reshape(128, 256))
    dvec = np.ascontiguousarray(np.asarray(L0_attn_D, np.float32).reshape(16, 128).T)
    wg = np.ascontiguousarray(
        np.asarray(L0_ffn_wg, np.float32).reshape(16, 128, 8)
        .transpose(1, 0, 2).reshape(128, 128))
    w5 = np.ascontiguousarray(
        np.asarray(L0_ffn_W, np.float32).reshape(8, 16, 128, 16, 128)
        .transpose(1, 0, 4, 3, 2).reshape(16, 8, 128, 2048))
    bias = np.ascontiguousarray(np.asarray(L0_ffn_b, np.float32))
    xs = [np.ascontiguousarray(np.asarray(L0_attn_x[i], np.float32)) for i in range(8)]
    common = {"wxp": wxp, "wbc": wbc, "alog": alog, "dvec": dvec,
              "wg": wg, "w5": w5, "bias": bias}
    return xs, common


def kernel(L0_attn_x, L0_attn_A_log, L0_attn_D, L0_attn_W_xproj,
           L0_ffn_wg, L0_ffn_W, L0_ffn_b):
    xs, common = prep_inputs(L0_attn_x, L0_attn_A_log, L0_attn_D, L0_attn_W_xproj,
                             L0_ffn_wg, L0_ffn_W, L0_ffn_b)
    nc = build_nc()
    in_maps = [dict(common, x=xs[i]) for i in range(8)]
    res = run_bass_kernel_spmd(nc, in_maps, core_ids=list(range(8)))
    out = np.stack([np.asarray(res.results[i]["y"], np.float32) for i in range(8)], axis=0)
    return out


if __name__ == "__main__":
    nc = build_nc()
    print("built ok, instructions:", nc.n_instructions())


# version 23
# speedup vs baseline: 1.3883x; 1.3883x over previous
import numpy as np
import concourse.bass as bass
import concourse.mybir as mybir
from concourse.bass_utils import run_bass_kernel_spmd

F32 = mybir.dt.float32
BF16 = mybir.dt.bfloat16
Alu = mybir.AluOpType
Act = mybir.ActivationFunctionType
AX = mybir.AxisListType.X

DM = 2048
LSEQ = 2048
NST = 16
NE = 8
NJ = 16
TS = 256
NCH = LSEQ // TS
TBT = 512
NTBG = LSEQ // TBT
EPS = 1e-5
RN = 1.0 / DM


class Sem:
    def __init__(self, nc, name):
        self._cm = nc.semaphore(name)
        self.h = self._cm.__enter__()
        self.v = 0


def build_nc():
    nc = bass.Bass("TRN2", target_bir_lowering=False)

    x_h = nc.dram_tensor("x", [LSEQ, DM], F32, kind="ExternalInput").ap()
    wxp_h = nc.dram_tensor("wxp", [16, 128, 2048], F32, kind="ExternalInput").ap()
    wbc_h = nc.dram_tensor("wbc", [16, 128, 32], F32, kind="ExternalInput").ap()
    alog_h = nc.dram_tensor("alog", [128, 256], F32, kind="ExternalInput").ap()
    dvec_h = nc.dram_tensor("dvec", [128, 16], F32, kind="ExternalInput").ap()
    wg_h = nc.dram_tensor("wg", [128, 128], F32, kind="ExternalInput").ap()
    b_h = nc.dram_tensor("bias", [8, 2048], F32, kind="ExternalInput").ap()
    w5_h = nc.dram_tensor("w5", [16, 8, 128, 2048], F32, kind="ExternalInput").ap()
    y_h = nc.dram_tensor("y", [LSEQ, DM], F32, kind="ExternalOutput").ap()
    wxpbf_s = nc.dram_tensor("wxpbf", [16, 128, 2048], BF16, kind="Internal").ap()
    wbf_s = nc.dram_tensor("wbf", [16, 8, 128, 2048], BF16, kind="Internal").ap()
    xp_s = nc.dram_tensor("xps", [LSEQ, DM], F32, kind="Internal").ap()
    ln2_s = nc.dram_tensor("ln2s", [NCH, 128, NJ * TS], BF16, kind="Internal").ap()

    def sb(name, shape, dt=F32):
        return nc.alloc_sbuf_tensor(name, shape, dt).ap()

    A_raw = sb("A_raw", [128, 256])
    A_sb = sb("A_sb", [128, 256])
    D_sb = sb("D_sb", [128, 16])
    wg_bf = sb("wg_bf", [128, 128], BF16)
    wxpbc_bf = sb("wxpbc", [128, 512], BF16)
    identf = sb("identf", [128, 128])
    identb = sb("identb", [128, 128], BF16)
    carry = sb("carry", [128, 512])
    masks = sb("masks", [128, 128])
    gate = sb("gate", [128, 16])
    flagb = sb("flagb", [128, 16])
    tmp16 = sb("tmp16", [128, 16])
    stats = sb("stats", [128, 64])
    lg_sb = sb("lg_sb", [8, 256])
    bc_sb = sb("bc_sb", [32, 256])
    bb_sb = sb("bb_sb", [128, 512])
    cb_sb = sb("cb_sb", [128, 512])
    wf32 = sb("wf32", [128, NJ * (TS + 1)])
    consts = sb("consts", [128, 8])

    xbuf0 = sb("xbuf0", [128, 2 * DM])
    xbuf1 = sb("xbuf1", [128, 2 * DM])
    tmbuf = sb("tmbuf", [128, NJ * (TS + 1)])
    fmbuf = sb("fmbuf", [128, NJ * TS])
    ln1fb = sb("ln1fb", [128, NJ * TS], BF16)
    dtb = sb("dtb", [128, NJ * TS])
    dtxb = sb("dtxb", [128, NJ * TS])
    ybuf = sb("ybuf", [128, NJ * TS])
    abuf = sb("abuf", [128, NJ * (TS + 1)])
    ubuf = sb("ubuf", [128, NJ * (TS + 1)])
    hbuf = sb("hbuf", [128, NJ * (TS + 1)])
    wxpK = sb("wxpK", [128, 2 * 2048], BF16)
    masksT = wxpK[0:8, 0:2048]
    b_bf = wxpK[0:8, 2048:4096]

    xe = [dtb[:].bitcast(BF16), dtxb[:].bitcast(BF16),
          ybuf[:].bitcast(BF16), abuf[:].bitcast(BF16)[:, 0:8192]]
    ln2TBG = ubuf[:].bitcast(BF16)[:, 0:8192]
    outFM_lo = hbuf
    outFM_hi = fmbuf
    wstage = xbuf0
    outtile = [xbuf1[:][:, 0:DM], xbuf1[:][:, DM:2 * DM]]
    xptile = [tmbuf[:][:, 0:DM], tmbuf[:][:, DM:2 * DM]]
    otmp = [ln1fb[:].bitcast(F32)[:, 0:512], ln1fb[:].bitcast(F32)[:, 512:1024]]
    meb_bf = ln1fb[:][:, 2048:2560]

    def wslot(i):
        v = wstage[:].bitcast(BF16)
        return v[:, i * 2048:(i + 1) * 2048]

    def outFM(mc):
        t = outFM_lo if mc < 8 else outFM_hi
        m = mc % 8
        return t[:][:, m * 512:(m + 1) * 512]

    ps_t = nc.alloc_psum_tensor("ps", [128, 4096], F32)
    ps = ps_t.ap()
    psb = ps.bitcast(BF16)

    def bank(b, lo=0, hi=512, parts=128):
        return ps[0:parts, b * 512 + lo: b * 512 + hi]

    def bankb(b, lo=0, hi=1024, parts=128):
        return psb[0:parts, b * 1024 + lo: b * 1024 + hi]

    sLD = Sem(nc, "sLD")
    sGP = Sem(nc, "sGP")
    sPC = Sem(nc, "sPC")
    sIO = Sem(nc, "sIO")
    sWX = [Sem(nc, "sWX0"), Sem(nc, "sWX1")]
    sXL = [Sem(nc, "sXL0"), Sem(nc, "sXL1")]
    sL2 = Sem(nc, "sL2")
    sLB = Sem(nc, "sLB")
    sWB = [Sem(nc, "sWB0"), Sem(nc, "sWB1")]
    sXP = [Sem(nc, "sXP0"), Sem(nc, "sXP1")]
    sOU = [Sem(nc, "sOU0"), Sem(nc, "sOU1")]
    sV = Sem(nc, "sV")
    sS = Sem(nc, "sS")
    sT = Sem(nc, "sT")
    sG = Sem(nc, "sG")

    ops = {"sync": [], "vector": [], "scalar": [], "tensor": [], "gpsimd": []}

    def emit(engn, maker, waits=(), inc=None, amt=1):
        wl = tuple((s.h, v) for (s, v) in waits if v > 0)
        if inc is not None:
            inc.v += amt
        ih = None if inc is None else inc.h

        def f(e, wl=wl, maker=maker, ih=ih, amt=amt):
            for (h, v) in wl:
                e.wait_ge(h, v)
            ins = maker(e)
            if ih is not None:
                ins.then_inc(ih, amt)
        ops[engn].append(f)
        return None if inc is None else inc.v

    def dma(engn, out, in_, waits=(), sem=None):
        sem = sem if sem is not None else (sGP if engn == "gpsimd" else sLD)
        return emit(engn, lambda e, o=out, i=in_: e.dma_start(out=o, in_=i),
                    waits, inc=sem, amt=16)

    cp = {}

    CONST_VALS = [0.0, 1.0, -1.0, -0.5, EPS, RN]
    for ci, val in enumerate(CONST_VALS):
        emit("gpsimd", lambda e, ci=ci, val=val: e.memset(consts[:, ci:ci + 1], val))
        nc.const_aps.register(F32, val, consts[:, ci:ci + 1])
    emit("gpsimd", lambda e: e.memset(identf, 0.0))
    emit("gpsimd", lambda e: e.affine_select(
        out=identf, in_=identf, compare_op=Alu.not_equal, fill=1.0, base=0,
        pattern=[[-1, 128]], channel_multiplier=1))
    emit("gpsimd", lambda e: e.memset(identb, 0.0))
    emit("gpsimd", lambda e: e.affine_select(
        out=identb, in_=identb, compare_op=Alu.not_equal, fill=1.0, base=0,
        pattern=[[-1, 128]], channel_multiplier=1))
    emit("gpsimd", lambda e: e.memset(carry, 0.0))
    emit("gpsimd", lambda e: e.memset(abuf, 0.0))
    emit("gpsimd", lambda e: e.memset(flagb, 1.0), inc=sG)
    cp["gp_pro"] = sG.v

    cp["alog_ld"] = dma("sync", A_raw, alog_h, sem=sIO)
    cp["dvec_ld"] = dma("sync", D_sb, dvec_h, waits=[(sIO, cp["alog_ld"])], sem=sIO)
    v1 = dma("sync", wf32[:, 0:128], wg_h, waits=[(sIO, cp["dvec_ld"])], sem=sIO)
    v2 = dma("sync", wf32[:, 128:640].rearrange("p (a b) -> p a b", a=16),
             wbc_h.transpose([1, 0, 2]), waits=[(sIO, v1)], sem=sIO)
    cp["wg_ld"] = emit("gpsimd", lambda e: e.tensor_copy(out=wg_bf, in_=wf32[:, 0:128]),
                       waits=[(sIO, v1)], inc=sG)
    cp["wbc_ld"] = emit("gpsimd", lambda e: e.tensor_copy(out=wxpbc_bf, in_=wf32[:, 128:640]),
                        waits=[(sIO, v2)], inc=sG)

    cp["A_exp"] = emit("scalar", lambda e: e.activation(out=A_sb, in_=A_raw, func=Act.Exp),
                       waits=[(sIO, cp["alog_ld"])], inc=sS)
    cp["A_neg"] = emit("vector", lambda e: e.tensor_scalar_mul(out=A_sb, in0=A_sb, scalar1=-1.0),
                       waits=[(sS, cp["A_exp"])], inc=sV)

    pslot = wf32[:].bitcast(BF16)
    NPS = 8
    pu = [(wxp_h[k][:, hf * 1024:(hf + 1) * 1024],
           wxpbf_s[k][:, hf * 1024:(hf + 1) * 1024])
          for k in range(16) for hf in range(2)]
    pu += [(w5_h[mc, e][:, hf * 1024:(hf + 1) * 1024],
            wbf_s[mc, e][:, hf * 1024:(hf + 1) * 1024])
           for mc in range(16) for e in range(8) for hf in range(2)]
    ldv = {}
    stv = {}
    for i, (psrc, pdst) in enumerate(pu):
        slot = pslot[:, (i % NPS) * 1024:((i % NPS) + 1) * 1024]
        wts = [(sGP, stv[i - NPS])] if i - NPS >= 0 else []
        ldv[i] = dma("gpsimd", slot, psrc, waits=wts)
        stv[i] = dma("gpsimd", pdst, slot, waits=[(sGP, ldv[i])])
        if i == 31:
            cp["wxp_pc"] = stv[i]
    cp["w_pc"] = sGP.v

    xbufs = [xbuf0, xbuf1]

    for c in range(NCH):
        xb = xbufs[c % 2][:]
        w_x = []
        if c >= 2:
            w_x = [(sV, cp[("xdoneV", c - 2)]), (sS, cp[("xdoneS", c - 2)]),
                   (sLD, cp[("xpst", c - 2, 1)])]
        for ts in range(2):
            g = 2 * c + ts
            cp[("xld", c, ts)] = dma(
                "sync", xb[:, ts * DM:(ts + 1) * DM], x_h[g * 128:(g + 1) * 128, :],
                waits=w_x, sem=sXL[ts])

        w_h = [(sT, cp[("xproj", c - 1)])] if c >= 1 else []
        for ts in range(2):
            xt = xb[:, ts * DM:(ts + 1) * DM]
            dummy = ln1fb[:][:, ts * DM:(ts + 1) * DM]
            sc = 16 * ts
            emit("scalar", lambda e, xt=xt, d=dummy, o=stats[:, sc:sc + 1]:
                 e.activation(out=d, in_=xt, func=Act.Copy, accum_out=o),
                 waits=[(sXL[ts], cp[("xld", c, ts)])] + w_h, inc=sS)
            cp[("sum", c, ts)] = emit(
                "scalar", lambda e, xt=xt, d=dummy, o=stats[:, sc + 1:sc + 2]:
                e.activation(out=d, in_=xt, func=Act.Square, accum_out=o), inc=sS)

        def stat_chain(base, sumkey, invkey, nmikey):
            for ts in range(2):
                sc = base + 16 * ts
                emit("vector", lambda e, sc=sc: e.tensor_scalar(
                    out=stats[:, sc + 2:sc + 3], in0=stats[:, sc:sc + 1],
                    scalar1=RN, scalar2=None, op0=Alu.mult),
                    waits=[(sS, cp[(sumkey, c, ts)])], inc=sV)
                emit("vector", lambda e, sc=sc: e.tensor_scalar(
                    out=stats[:, sc + 3:sc + 4], in0=stats[:, sc + 1:sc + 2],
                    scalar1=RN, scalar2=None, op0=Alu.mult), inc=sV)
            vA = sV.v
            for ts in range(2):
                sc = base + 16 * ts
                emit("vector", lambda e, sc=sc: e.tensor_mul(
                    out=stats[:, sc + 4:sc + 5], in0=stats[:, sc + 2:sc + 3],
                    in1=stats[:, sc + 2:sc + 3]),
                    waits=[(sV, vA)] if ts == 0 else (), inc=sV)
            vB = sV.v
            for ts in range(2):
                sc = base + 16 * ts
                emit("vector", lambda e, sc=sc: e.tensor_sub(
                    out=stats[:, sc + 5:sc + 6], in0=stats[:, sc + 3:sc + 4],
                    in1=stats[:, sc + 4:sc + 5]),
                    waits=[(sV, vB)] if ts == 0 else (), inc=sV)
            vC = sV.v
            for ts in range(2):
                sc = base + 16 * ts
                emit("scalar", lambda e, sc=sc: e.activation(
                    out=stats[:, sc + 6:sc + 7], in_=stats[:, sc + 5:sc + 6],
                    func=Act.Ln, bias=EPS),
                    waits=([(sV, vC), (sG, cp["gp_pro"])] if ts == 0 else ()), inc=sS)
            for ts in range(2):
                sc = base + 16 * ts
                cp[(invkey, c, ts)] = emit("scalar", lambda e, sc=sc: e.activation(
                    out=stats[:, sc + 7:sc + 8], in_=stats[:, sc + 6:sc + 7],
                    func=Act.Exp, scale=-0.5), inc=sS)
            for ts in range(2):
                sc = base + 16 * ts
                cp[(nmikey, c, ts)] = emit("vector", lambda e, sc=sc: e.scalar_tensor_tensor(
                    out=stats[:, sc + 8:sc + 9], in0=stats[:, sc + 2:sc + 3],
                    scalar=-1.0, in1=stats[:, sc + 7:sc + 8],
                    op0=Alu.mult, op1=Alu.mult),
                    waits=[(sS, cp[(invkey, c, 1)])] if ts == 0 else (), inc=sV)

        stat_chain(0, "sum", "inv", "nmi")
        for ts in range(2):
            sc = 16 * ts
            xt = xb[:, ts * DM:(ts + 1) * DM]
            wts = [(sV, cp[("nmi", c, ts)])]
            if ts == 0 and c >= 1:
                wts.append((sV, cp[("scanf", c - 1)]))
            cp[("ln1tm", c, ts)] = emit(
                "scalar", lambda e, xt=xt, sc=sc, o=tmbuf[:][:, ts * DM:(ts + 1) * DM]:
                e.activation(out=o, in_=xt, func=Act.Identity,
                             scale=stats[:, sc + 7:sc + 8], bias=stats[:, sc + 8:sc + 9]),
                waits=wts, inc=sS)

        for grp in range(8):
            ts, jq = grp // 4, grp % 4
            bk = 6 + grp % 2
            for k in range(4):
                j = jq * 4 + k
                src = tmbuf[:][:, ts * DM + j * 128: ts * DM + (j + 1) * 128]
                wts = []
                if k == 0:
                    wts.append((sS, cp[("ln1tm", c, ts)]))
                    if grp >= 2:
                        wts.append((sS, cp[("trev", c, grp - 2)]))
                    elif c >= 1:
                        wts.append((sS, cp[("spfree", c - 1, 4 + grp % 2)]))
                        wts.append((sS, cp[("spfree", c - 1, 6 + grp % 2)]))
                    if grp == 0:
                        wts.append((sG, cp["gp_pro"]))
                emit("tensor", lambda e, s=src, d=bank(bk, k * 128, (k + 1) * 128):
                     e.transpose(d, s, identf), waits=wts, inc=sT)
            vtr = sT.v
            o32 = fmbuf[:].rearrange("p (a b) -> p a b", a=NJ)[:, jq * 4:(jq + 1) * 4,
                                                              ts * 128:(ts + 1) * 128]
            ob = ln1fb[:].rearrange("p (a b) -> p a b", a=NJ)[:, jq * 4:(jq + 1) * 4,
                                                              ts * 128:(ts + 1) * 128]
            wev = [(sT, vtr)]
            if grp == 0 and c >= 1:
                wev.append((sL2, cp[("l2st", c - 1)]))
            emit("scalar", lambda e, o=o32, b2=bank(bk): e.copy(out=o, in_=b2),
                 waits=wev, inc=sS)
            cp[("trev", c, grp)] = emit(
                "scalar", lambda e, o=ob, b2=bank(bk): e.copy(out=o, in_=b2), inc=sS)
        cp[("ln1fm", c)] = sS.v

        wbc0 = [(sS, cp[("ln1fm", c)]), (sG, cp["wbc_ld"])]
        if c >= 1:
            wbc0.append((sV, cp[("ytrev", c - 1)]))
        for k in range(16):
            emit("tensor", lambda e, k=k: e.matmul(
                bank(2, 0, 256, parts=32),
                wxpbc_bf[:, k * 32:(k + 1) * 32],
                ln1fb[:][:, k * TS:(k + 1) * TS],
                start=(k == 0), stop=(k == 15)),
                waits=(wbc0 if k == 0 else ()), inc=sT)
        cp[("bcmm", c)] = sT.v
        cp[("bcev", c)] = emit("scalar",
                               lambda e: e.copy(out=bc_sb, in_=bank(2, 0, 256, parts=32)),
                               waits=[(sT, cp[("bcmm", c)])], inc=sS)

        def wxp_load(k):
            wts = [(sGP, cp["wxp_pc"])]
            kk = ("xpK", c, k - 2) if k >= 2 else (("xpK", c - 1, 14 + k) if c >= 1 else None)
            if kk is not None:
                wts.append((sT, cp[kk]))
            cp[("wxk", c, k)] = dma(
                "sync", wxpK[:, (k % 2) * 2048:((k % 2) + 1) * 2048],
                wxpbf_s[k], waits=wts, sem=sWX[k % 2])

        wxp_load(0)
        wxp_load(1)
        for k in range(16):
            for mc in range(16):
                wts = []
                if mc == 0:
                    wts.append((sWX[k % 2], cp[("wxk", c, k)]))
                    if k == 0:
                        wts.append((sS, cp[("bcev", c)]))
                        if c >= 1:
                            wts.append((sS, cp[("spfree_all", c - 1)]))
                            wts.append((sV, cp[("msk", c - 1, 1)]))
                emit("tensor", lambda e, k=k, mc=mc: e.matmul(
                    ps[:, mc * 256:(mc + 1) * 256],
                    wxpK[:, (k % 2) * 2048 + mc * 128:(k % 2) * 2048 + (mc + 1) * 128],
                    ln1fb[:][:, k * TS:(k + 1) * TS],
                    start=(k == 0 and mc % 2 == 0), stop=(k == 15)),
                    waits=wts, inc=sT)
            cp[("xpK", c, k)] = sT.v
            if k + 2 < 16:
                wxp_load(k + 2)
        cp[("xproj", c)] = sT.v

        SLOT = lambda q, i: hbuf[:][:, q * 2048 + i * 256: q * 2048 + (i + 1) * 256]
        for p in range(8):
            q = p % 2
            mcs = (2 * p, 2 * p + 1)
            for i, mc in enumerate(mcs):
                wts = []
                if p == 0 and i == 0:
                    wts = [(sT, cp[("xproj", c)])]
                    if c >= 1:
                        wts.append((sV, cp[("w", c - 1, 15)]))
                if p >= 2 and i == 0:
                    wts.append((sV, cp[("dtadd", c, 2 * (p - 2) + 1)]))
                emit("scalar", lambda e, mc=mc, o=SLOT(q, 0 + i): e.activation(
                    out=o, in_=ps[:, mc * 256:(mc + 1) * 256], func=Act.Abs),
                    waits=wts, inc=sS)
            for i, mc in enumerate(mcs):
                emit("scalar", lambda e, mc=mc, o=SLOT(q, 2 + i): e.activation(
                    out=o, in_=ps[:, mc * 256:(mc + 1) * 256], func=Act.Relu), inc=sS)
            cp[("spfree", c, p)] = sS.v
            for i in range(2):
                emit("scalar", lambda e, i=i, q=q: e.activation(
                    out=SLOT(q, 4 + i), in_=SLOT(q, 0 + i), func=Act.Exp, scale=-1.0), inc=sS)
            for i in range(2):
                emit("scalar", lambda e, i=i, q=q: e.activation(
                    out=SLOT(q, 6 + i), in_=SLOT(q, 4 + i), func=Act.Ln, bias=1.0), inc=sS)
            vsp = sS.v
            for i, mc in enumerate(mcs):
                cp[("dtadd", c, mc)] = emit(
                    "vector", lambda e, mc=mc, i=i, q=q: e.tensor_add(
                        out=dtb[:][:, mc * 256:(mc + 1) * 256],
                        in0=SLOT(q, 6 + i), in1=SLOT(q, 2 + i)),
                    waits=[(sS, vsp)], inc=sV)
        cp[("spfree_all", c)] = sS.v
        cp[("dt", c)] = sV.v
        cp[("dtx", c)] = emit("vector", lambda e: e.tensor_mul(
            out=dtxb, in0=dtb, in1=fmbuf),
            waits=[(sV, cp[("dt", c)])], inc=sV)
        for j in range(NJ):
            wts = []
            if j == 0:
                wts = [(sV, cp[("dtx", c)]), (sIO, cp["dvec_ld"])]
                if c >= 1:
                    wts.append((sV, cp[("ytrev", c - 1)]))
                    wts.append(cp[("ygf", c - 1)])
            cp[("yini", c, j)] = emit("scalar", lambda e, j=j: e.activation(
                out=ybuf[:][:, j * TS:(j + 1) * TS],
                in_=fmbuf[:][:, j * TS:(j + 1) * TS],
                func=Act.Identity, scale=D_sb[:, j:j + 1]),
                waits=wts, inc=sS)
        cp[("yini", c)] = sS.v

        SW = TS + 1
        a_bufs = [abuf[:], tmbuf[:][:, 0:NJ * SW]]
        a3 = [b.rearrange("p (a b) -> p a b", a=NJ) for b in a_bufs]
        GPY = c >= 3
        YGP = False
        u_bufs = [ubuf[:][:, 0:NJ * SW],
                  (wf32[:][:, 0:NJ * SW] if GPY else ubuf[:][:, 0:NJ * SW])]
        u3p = [b.rearrange("p (a b) -> p a b", a=NJ) for b in u_bufs]
        h3 = hbuf[:][:, 0:NJ * SW].rearrange("p (a b) -> p a b", a=NJ)
        d3 = lambda ap: ap.rearrange("p (a b) -> p a b", a=NJ)
        hblk = h3[:, :, 1:SW]


        def selmm(n):
            par = n % 2
            wts = [(sS, cp[("bcev", c)]), (sS, cp[("spfree", c, par)])]
            if n >= 2:
                wts.append((sS, cp[("bcbev", c, n - 2)]))
            emit("tensor", lambda e, n=n, par=par: e.matmul(
                bank(par, 0, 256),
                identf[0:32, n:n + 1].broadcast_to([32, 128]), bc_sb,
                start=True, stop=False), waits=wts, inc=sT)
            cp[("selmm", c, n)] = emit("tensor", lambda e, n=n, par=par: e.matmul(
                bank(par, 256, 512),
                identf[0:32, (n + 16):(n + 17)].broadcast_to([32, 128]), bc_sb,
                start=False, stop=True), inc=sT)

        def bcbevac(n):
            par = n % 2
            bb = bb_sb[:, par * 256:(par + 1) * 256]
            cb = cb_sb[:, par * 256:(par + 1) * 256]
            wev = [(sT, cp[("selmm", c, n)])]
            if n >= 2:
                wev.append((sV, cp[("u", c, n - 2)]))
                wev.append((sV, cp[("w", c, n - 2)]))
            emit("scalar", lambda e, bb=bb, par=par: e.copy(out=bb, in_=bank(par, 0, 256)),
                 waits=wev, inc=sS)
            cp[("bcbev", c, n)] = emit(
                "scalar", lambda e, cb=cb, par=par: e.copy(out=cb, in_=bank(par, 256, 512)),
                inc=sS)

        def agen(n):
            par = n % 2
            ab = a_bufs[par]
            for j in range(NJ):
                wts = []
                if j == 0:
                    wts = [(sV, cp[("dt", c)])]
                    if n >= 2:
                        wts.append((sV, cp[("scans", c, n - 2)]))
                    elif c >= 1:
                        wts.append((sV, cp[("scans", c - 1, 14 + n)]))
                    if par == 1:
                        wts.append((sT, cp[("bcmm", c)]))
                cp[("a", c, n, j)] = emit("scalar", lambda e, j=j, n=n, ab=ab: e.activation(
                    out=ab[:, j * SW + 1:(j + 1) * SW],
                    in_=dtb[:][:, j * TS:(j + 1) * TS],
                    func=Act.Exp, scale=A_sb[:, (j * NST + n):(j * NST + n + 1)]),
                    waits=wts, inc=sS)
            cp[("a", c, n)] = sS.v

        def v_steps(n):
            par = n % 2
            bb = bb_sb[:, par * 256:(par + 1) * 256]
            cb = cb_sb[:, par * 256:(par + 1) * 256]
            ab = a_bufs[par]
            u3 = u3p[par]
            ublk = u3[:, :, 1:SW]
            wts = [(sS, cp[("bcbev", c, n)])]
            if n == 0:
                wts.append((sV, cp[("dtx", c)]))
            if GPY and c == 3 and n <= 1 and par == 1:
                wts.append((sGP, cp["w_pc"]))
            ykey = ("yg", c, n - 2) if n >= 2 else (("yg", c - 1, 14 + n) if c >= 1 else None)
            if ykey is not None and ykey in cp:
                wts.append(cp[ykey])
            emit("vector", lambda e, bb=bb, ublk=ublk: e.tensor_tensor(
                out=ublk, in0=d3(dtxb[:]),
                in1=bb.unsqueeze(1).broadcast_to([128, NJ, TS]), op=Alu.mult),
                waits=wts, inc=sV)
            emit("vector", lambda e, ab=ab: e.memset(
                ab.rearrange("p (a b) -> p a b", a=NJ)[:, :, 0:1], 0.0),
                waits=([(sT, cp[("bcmm", c)])] if n == 1 else ()), inc=sV)
            rpar = ((c + 1) % 2) * 256
            cp[("u", c, n)] = emit("vector", lambda e, n=n, rpar=rpar, u3=u3: e.tensor_copy(
                out=u3[:, :, 0:1],
                in_=carry[:, rpar + n:rpar + 256:NST].unsqueeze(2)),
                waits=([(sG, cp["gp_pro"])] if (c == 0 and n == 0) else ()), inc=sV)
            wts = [(sS, cp[("a", c, n)]), (sV, cp[("u", c, n)])]
            if c == 0 and n == 0:
                wts.append((sG, cp["gp_pro"]))
            cp[("scan", c, n)] = emit(
                "vector", lambda e, ab=ab, ub=u_bufs[par]: e.tensor_tensor_scan(
                    out=hbuf[:][:, 0:NJ * SW], data0=ab[:, 0:NJ * SW],
                    data1=ub, initial=0.0,
                    op0=Alu.mult, op1=Alu.add),
                waits=wts, inc=sV)
            cp[("scans", c, n)] = sV.v
            wpar = (c % 2) * 256
            cp[("cry", c, n)] = emit("vector", lambda e, n=n, wpar=wpar: e.tensor_copy(
                out=carry[:, wpar + n:wpar + 256:NST].unsqueeze(2),
                in_=h3[:, :, 256:257]),
                waits=[(sV, cp[("scans", c, n)])], inc=sV)
            cp[("w", c, n)] = emit("vector", lambda e, cb=cb, ublk=ublk: e.tensor_tensor(
                out=ublk, in0=hblk,
                in1=cb.unsqueeze(1).broadcast_to([128, NJ, TS]), op=Alu.mult),
                waits=[(sV, cp[("scans", c, n)])], inc=sV)
            wts = [(sV, cp[("w", c, n)])]
            if n == 0:
                wts.append((sS, cp[("yini", c)]))
            if GPY and YGP:
                cp[("yg", c, n)] = (sG, emit("gpsimd", lambda e, ublk=ublk: e.tensor_tensor(
                    out=d3(ybuf[:]), in0=d3(ybuf[:]), in1=ublk, op=Alu.add),
                    waits=wts, inc=sG))
            else:
                cp[("yg", c, n)] = (sV, emit("vector", lambda e, ublk=ublk: e.tensor_tensor(
                    out=d3(ybuf[:]), in0=d3(ybuf[:]), in1=ublk, op=Alu.add),
                    waits=wts, inc=sV))

        selmm(0)
        agen(0)
        bcbevac(0)
        selmm(1)
        agen(1)
        for n in range(NST):
            if n + 2 < NST:
                selmm(n + 2)
            v_steps(n)
            if n + 1 < NST:
                bcbevac(n + 1)
            if n + 2 < NST:
                agen(n + 2)
        cp[("scan_done", c)] = cp[("scans", c, 15)]
        cp[("scanf", c)] = cp[("scans", c, 15)]
        cp[("ygf", c)] = cp[("yg", c, 15)]

        for grp in range(8):
            ts, jq = grp // 4, grp % 4
            bk = 2 + grp % 2
            for k in range(4):
                j = jq * 4 + k
                src = ybuf[:][:, j * TS + ts * 128: j * TS + (ts + 1) * 128]
                wts = []
                if k == 0:
                    wts.append(cp[("ygf", c)])
                    wts.append((sS, cp[("spfree_all", c)]))
                    if grp >= 2:
                        wts.append((sV, cp[("ytr", c, grp - 2)]))
                emit("tensor", lambda e, s=src, d=bank(bk, k * 128, (k + 1) * 128):
                     e.transpose(d, s, identf), waits=wts, inc=sT)
            vtr = sT.v
            xs = xb[:, ts * DM + jq * 512: ts * DM + (jq + 1) * 512]
            cp[("ytr", c, grp)] = emit("vector", lambda e, b2=bank(bk), xs=xs:
                                       e.tensor_add(out=xs, in0=b2, in1=xs),
                                       waits=[(sT, vtr)], inc=sV)
        cp[("ytrev", c)] = sV.v
        cp[("xdoneV", c)] = sV.v
        for ts in range(2):
            cp[("xpst", c, ts)] = dma("sync", xp_s[(2 * c + ts) * 128:(2 * c + ts + 1) * 128, :],
                                      xb[:, ts * DM:(ts + 1) * DM],
                                      waits=[(sV, cp[("ytrev", c)])])
        ln2tm = fmbuf[:].bitcast(BF16)[:, 0:2 * DM]
        for ts in range(2):
            xpt = xb[:, ts * DM:(ts + 1) * DM]
            dummy = ln1fb[:][:, ts * DM:(ts + 1) * DM]
            sc = 32 + 16 * ts
            emit("scalar", lambda e, xt=xpt, d=dummy, o=stats[:, sc:sc + 1]:
                 e.activation(out=d, in_=xt, func=Act.Copy, accum_out=o),
                 waits=[(sV, cp[("ytrev", c)])] if ts == 0 else (), inc=sS)
            cp[("sum2", c, ts)] = emit(
                "scalar", lambda e, xt=xpt, d=dummy, o=stats[:, sc + 1:sc + 2]:
                e.activation(out=d, in_=xt, func=Act.Square, accum_out=o), inc=sS)
        stat_chain(32, "sum2", "inv2", "nmi2")
        for ts in range(2):
            sc = 32 + 16 * ts
            xpt = xb[:, ts * DM:(ts + 1) * DM]
            cp[("ln2tm", c, ts)] = emit(
                "scalar", lambda e, xt=xpt, sc=sc, o=ln2tm[:, ts * DM:(ts + 1) * DM]:
                e.activation(out=o, in_=xt, func=Act.Identity,
                             scale=stats[:, sc + 7:sc + 8], bias=stats[:, sc + 8:sc + 9]),
                waits=[(sV, cp[("nmi2", c, ts)])], inc=sS)
        ln2fm = fmbuf[:].bitcast(BF16)[:, 2 * DM:2 * DM + NJ * TS]
        for grp in range(4):
            ts, jh = grp // 2, grp % 2
            bk = 2 + grp % 2
            for k in range(8):
                j = jh * 8 + k
                src = ln2tm[:, ts * DM + j * 128: ts * DM + (j + 1) * 128]
                wts = []
                if k == 0:
                    wts.append((sS, cp[("ln2tm", c, ts)]))
                    wts.append((sV, cp[("ytrev", c)]))
                    if grp >= 2:
                        wts.append((sS, cp[("l2ev", c, grp - 2)]))
                emit("tensor", lambda e, s=src, d=bankb(bk, k * 128, (k + 1) * 128):
                     e.transpose(d, s, identb), waits=wts, inc=sT)
            vtr = sT.v
            ofm = ln2fm.rearrange("p (a b) -> p a b", a=NJ)[:, jh * 8:(jh + 1) * 8,
                                                            ts * 128:(ts + 1) * 128]
            cp[("l2ev", c, grp)] = emit(
                "scalar", lambda e, o=ofm, b2=bankb(bk): e.copy(out=o, in_=b2),
                waits=[(sT, vtr)], inc=sS)
        cp[("ln2fm", c)] = sS.v
        cp[("l2st", c)] = dma("sync", ln2_s[c], ln2fm, waits=[(sS, cp[("ln2fm", c)])],
                              sem=sL2)
        cp[("lntm_free", c)] = sS.v

        for k in range(16):
            wts = []
            if k == 0:
                wts = [(sS, cp[("ln2fm", c)]), (sG, cp["wg_ld"])]
                if c >= 1:
                    wts.append((sS, cp[("lgev", c - 1)]))
            emit("tensor", lambda e, k=k: e.matmul(
                bank(2, 0, 256, parts=8), wg_bf[:, k * 8:(k + 1) * 8],
                ln2fm[:, k * TS:(k + 1) * TS],
                start=(k == 0), stop=(k == 15)), waits=wts, inc=sT)
        vlg = sT.v
        cp[("lgev", c)] = emit("scalar",
                               lambda e: e.copy(out=lg_sb, in_=bank(2, 0, 256, parts=8)),
                               waits=[(sT, vlg)], inc=sS)
        for ts in range(2):
            g = 2 * c + ts
            lgT = bank(3, ts * 8, (ts + 1) * 8)
            wts = [(sS, cp[("lgev", c)])]
            if ts == 1:
                wts.append((sS, cp[("vex", c, 0)]))
                wts.append((sV, cp[("msk", c, 0)]))
            elif c >= 1:
                wts.append((sS, cp[("vex", c - 1, 1)]))
                wts.append((sV, cp[("msk", c - 1, 1)]))
            emit("tensor", lambda e, s=lg_sb[:, ts * 128:(ts + 1) * 128], d=lgT:
                 e.transpose(d, s, identf[0:8, 0:8]), waits=wts, inc=sT)
            vt = sT.v
            nmcol = stats[:, 60:61]
            secol = stats[:, 61:62]
            e8 = stats[:, 48:56]
            vnm = emit("vector", lambda e, lgT=lgT, o=nmcol: e.reduce_max(
                out=o, in_=lgT, axis=AX, negate=True),
                waits=[(sT, vt)], inc=sV)
            cp[("vex", c, ts)] = emit(
                "scalar", lambda e, lgT=lgT, o=e8, nm=nmcol, se=secol: e.activation(
                    out=o, in_=lgT, func=Act.Exp, bias=nm, accum_out=se),
                waits=[(sV, vnm)], inc=sS)
            emit("vector", lambda e, g=g, se=secol: e.reciprocal(
                out=gate[:, g:g + 1], in_=se), waits=[(sS, cp[("vex", c, ts)])], inc=sV)
            cp[("msk", c, ts)] = emit("vector", lambda e, g=g, lgT=lgT, nm=nmcol:
                                      e.tensor_scalar(
                                          out=masks[:, g * 8:(g + 1) * 8], in0=lgT,
                                          scalar1=nm, scalar2=0.0,
                                          op0=Alu.add, op1=Alu.is_ge),
                                      waits=[(sV, vnm)], inc=sV)
        cp[("xdoneS", c)] = sS.v

    vprev = cp[("msk", NCH - 1, 1)]
    for e_ in range(NE):
        mslice = masks[:, e_:128:8]
        v1 = emit("vector", lambda e, m=mslice: e.tensor_mul(out=m, in0=m, in1=flagb),
                  waits=[(sV, vprev), (sG, cp["gp_pro"])], inc=sV)
        v2 = emit("vector", lambda e, m=mslice: e.tensor_scalar(
            out=tmp16, in0=m, scalar1=-1.0, scalar2=1.0, op0=Alu.mult, op1=Alu.add),
            waits=[(sV, v1)], inc=sV)
        vprev = emit("vector", lambda e: e.tensor_mul(out=flagb, in0=flagb, in1=tmp16),
                     waits=[(sV, v2)], inc=sV)
    cp["tiebreak"] = vprev
    for grp in range(4):
        bk = 6 + grp % 2
        for k in range(4):
            tsg = grp * 4 + k
            src = masks[:, tsg * 8:(tsg + 1) * 8]
            wts = []
            if k == 0:
                wts.append((sV, cp["tiebreak"]))
                wts.append((sS, cp[("vex", NCH - 1, 1)]))
                if grp >= 2:
                    wts.append((sS, cp[("mtev", grp - 2)]))
            emit("tensor", lambda e, s=src, d=bank(bk, k * 128, (k + 1) * 128, parts=8):
                 e.transpose(d, s, identf), waits=wts, inc=sT)
        vtr = sT.v
        cp[("mtev", grp)] = emit(
            "scalar", lambda e, grp=grp, bk=bk: e.copy(
                out=masksT[:, grp * 512:(grp + 1) * 512],
                in_=bank(bk, 0, 512, parts=8)),
            waits=[(sT, vtr)], inc=sS)
    cp["masksT"] = sS.v
    cp["A_end_V"] = sV.v
    cp["A_end_S"] = sS.v
    cp["A_end_T"] = sT.v
    cp["A_end_G"] = sG.v

    def _prev_pair(tb, eh, mc, hp):
        idx = (((tb * 2 + eh) * 16 + mc) * 2 + hp) - 2
        if idx < 0:
            return None
        hpb = idx % 2
        mcb = (idx // 2) % 16
        ehb = (idx // 32) % 2
        tbb = idx // 64
        return (tbb, ehb, mcb, hpb)

    def _prev_unit(tb, eh, mc, e4, back):
        idx = ((tb * 2 + eh) * 16 + mc) * 4 + e4 - (2 if back == 2 else 4)
        if idx < 0:
            return None
        e4b = idx % 4
        mcb = (idx // 4) % 16
        ehb = (idx // 64) % 2
        tbb = idx // 128
        return (tbb, ehb, mcb, e4b)

    vb = dma("sync", wf32[0:8, 2048:4096], b_h,
             waits=[(sGP, cp["w_pc"]), (sT, cp["A_end_T"]), (sV, cp["A_end_V"]),
                    (sG, cp["A_end_G"]), (sS, cp["A_end_S"])])
    cp["b_ld"] = emit("gpsimd", lambda e: e.tensor_copy(out=b_bf, in_=wf32[0:8, 2048:4096]),
                      waits=[(sLD, vb)], inc=sG)
    NKC = 16
    last_meb_ev = 0
    last_xe = 0
    for tb in range(NTBG):
        wts = [(sV, cp["A_end_V"]), (sS, cp["A_end_S"]), (sT, cp["A_end_T"]),
               (sG, cp["A_end_G"])]
        if tb >= 1:
            wts.append((sT, cp[("mmall", tb - 1)]))
        for hf in range(2):
            cp[("l2ld", tb, hf)] = dma(
                "sync", ln2TBG[:, hf * 4096:(hf + 1) * 4096], ln2_s[tb * 2 + hf],
                waits=(wts if hf == 0 else []), sem=sLB)
        for eh in range(2):
            for e4 in range(4):
                ex = eh * 4 + e4
                wts = [(sS, cp["masksT"])]
                if last_meb_ev:
                    wts.append((sS, last_meb_ev))
                vmm = emit("tensor", lambda e, ex=ex, tb=tb: e.matmul(
                    bank(2), identb[0:8, ex:ex + 1].broadcast_to([8, 128]),
                    masksT[:, tb * 512:(tb + 1) * 512],
                    start=True, stop=True), waits=wts, inc=sT)
                wev = [(sT, vmm)]
                if last_xe:
                    wev.append((sV, last_xe))
                last_meb_ev = emit("scalar", lambda e: e.copy(out=meb_bf, in_=bank(2)),
                                   waits=wev, inc=sS)
                wxe = [(sS, last_meb_ev), (sLB, cp[("l2ld", tb, 1)])]
                if tb > 0 or eh > 0:
                    prev = ("mmeh", tb, eh - 1) if eh > 0 else ("mmeh", tb - 1, 1)
                    wxe.append((sT, cp[prev]))
                src0 = ln2TBG.rearrange("p (b a c) -> p a b c", b=2, a=NJ)
                src1 = meb_bf.rearrange("p (b c) -> p b c", b=2).unsqueeze(1)\
                    .broadcast_to([128, NJ, 2, 256])
                dst = xe[e4].rearrange("p (a b c) -> p a b c", a=NJ, b=2)
                last_xe = emit(
                    "vector", lambda e, d=dst, s0=src0, s1=src1: e.tensor_tensor(
                        out=d, in0=s0, in1=s1, op=Alu.mult),
                    waits=wxe, inc=sV)
                cp[("xe", tb, eh, e4)] = last_xe
            for mc in range(16):
                wsl4 = wstage[:].bitcast(BF16).rearrange("p (a b) -> p a b", a=4)
                for hp in range(2):
                    wts = [(sGP, cp["w_pc"])]
                    pkey = ("wld_prev", tb, eh, mc, hp)
                    prev = _prev_pair(tb, eh, mc, hp)
                    if prev is not None:
                        wts.append((sT, cp[("wsl",) + prev[:3] + (prev[3] * 2 + 1,)]))
                    vld = dma("sync", wsl4[:, hp * 2:(hp + 1) * 2],
                              wbf_s[mc, eh * 4 + hp * 2: eh * 4 + hp * 2 + 2]
                              .transpose([1, 0, 2]),
                              waits=wts, sem=sWB[hp])
                    cp[("wld", tb, eh, mc, hp * 2)] = vld
                    cp[("wld", tb, eh, mc, hp * 2 + 1)] = vld
                bk = mc % 2
                if eh == 0:
                    wts = [(sS, cp["masksT"]), (sG, cp["b_ld"])]
                    if mc >= 2:
                        kf = cp[("evB", tb, 0, mc - 2)]
                        wts.append((kf[0], kf[1]))
                    elif tb >= 1:
                        kf = cp[("evB", tb - 1, 1, 14 + mc)]
                        wts.append((kf[0], kf[1]))
                    emit("tensor", lambda e, mc=mc, tb=tb, bk=bk: e.matmul(
                        bank(bk), b_bf[:, mc * 128:(mc + 1) * 128],
                        masksT[:, tb * 512:(tb + 1) * 512],
                        start=True, stop=False), waits=wts, inc=sT)
                for e4 in range(4):
                    si = (mc * 4 + e4) % 4
                    for k in range(NKC):
                        wts = []
                        if k == 0:
                            wts.append((sWB[e4 // 2], cp[("wld", tb, eh, mc, e4)]))
                            wts.append((sV, cp[("xe", tb, eh, e4)]))
                            if e4 == 0 and eh == 1:
                                if mc >= 2:
                                    kf = cp[("evB", tb, 1, mc - 2)]
                                else:
                                    kf = cp[("evB", tb, 0, 14 + mc)]
                                wts.append((kf[0], kf[1]))
                        emit("tensor", lambda e, si=si, k=k, e4=e4, bk=bk, eh=eh:
                             e.matmul(
                                 bank(bk),
                                 wslot(si)[:, k * 128:(k + 1) * 128],
                                 xe[e4][:, k * 512:(k + 1) * 512],
                                 start=(eh == 1 and e4 == 0 and k == 0),
                                 stop=(e4 == 3 and k == NKC - 1)),
                             waits=wts, inc=sT)
                    cp[("wsl", tb, eh, mc, e4)] = sT.v
                vmm = sT.v
                if eh == 0:
                    cp[("evB", tb, 0, mc)] = (sS, emit(
                        "scalar", lambda e, mc=mc, bk=bk: e.copy(out=outFM(mc), in_=bank(bk)),
                        waits=[(sT, vmm)], inc=sS))
                else:
                    kf = cp[("evB", tb, 0, mc)]
                    cp[("evB", tb, 1, mc)] = (sV, emit(
                        "vector", lambda e, mc=mc, bk=bk: e.tensor_add(
                            out=outFM(mc), in0=outFM(mc), in1=bank(bk)),
                        waits=[(sT, vmm), (kf[0], kf[1])], inc=sV))
            cp[("mmeh", tb, eh)] = sT.v
        cp[("mmall", tb)] = sT.v
        for ts in range(4):
            gidx = tb * 4 + ts
            wxp = [(sLD, cp[("xpst", gidx // 2, gidx % 2)])]
            if tb == 0 and ts < 2:
                wxp += [(sV, cp["A_end_V"]), (sS, cp["A_end_S"])]
            if ts >= 2:
                wxp.append((sOU[ts % 2], cp[("outst", tb, ts - 2)]))
            elif tb >= 1:
                wxp.append((sOU[ts % 2], cp[("outst", tb - 1, ts + 2)]))
            cp[("xpl", tb, ts)] = dma(
                "gpsimd", xptile[ts % 2], xp_s[gidx * 128:(gidx + 1) * 128, :],
                waits=wxp, sem=sXP[ts % 2])
            for grp in range(4):
                bk = 3 + grp % 2
                for k in range(4):
                    mc = grp * 4 + k
                    src = outFM(mc)[:, ts * 128:(ts + 1) * 128]
                    wts = []
                    if k == 0:
                        kf = cp[("evB", tb, 1, grp * 4 + 3)]
                        wts.append((kf[0], kf[1]))
                        if grp >= 2:
                            wts.append((sV, cp[("otr", tb, ts, grp - 2)]))
                        elif ts >= 1:
                            wts.append((sV, cp[("otr", tb, ts - 1, grp + 2)]))
                        elif tb >= 1:
                            wts.append((sV, cp[("otr", tb - 1, 3, grp + 2)]))
                    emit("tensor", lambda e, s=src, d=bank(bk, k * 128, (k + 1) * 128):
                         e.transpose(d, s, identf), waits=wts, inc=sT)
                vtr = sT.v
                vsc = emit("scalar", lambda e, gidx=gidx, bk=bk, o=otmp[grp % 2]:
                           e.activation(out=o, in_=bank(bk), func=Act.Identity,
                                        scale=gate[:, gidx:gidx + 1]),
                           waits=[(sT, vtr)], inc=sS)
                wadd = [(sS, vsc), (sXP[ts % 2], cp[("xpl", tb, ts)])]
                if grp == 0:
                    if ts >= 2:
                        wadd.append((sOU[ts % 2], cp[("outst", tb, ts - 2)]))
                    elif tb >= 1:
                        wadd.append((sOU[ts % 2], cp[("outst", tb - 1, ts + 2)]))
                cp[("otr", tb, ts, grp)] = emit(
                    "vector", lambda e, ts=ts, grp=grp, o=otmp[grp % 2]: e.tensor_add(
                        out=outtile[ts % 2][:, grp * 512:(grp + 1) * 512],
                        in0=o, in1=xptile[ts % 2][:, grp * 512:(grp + 1) * 512]),
                    waits=wadd, inc=sV)
            cp[("outst", tb, ts)] = dma(
                "gpsimd", y_h[gidx * 128:(gidx + 1) * 128, :], outtile[ts % 2],
                waits=[(sV, cp[("otr", tb, ts, 3)])], sem=sOU[ts % 2])

    emit("gpsimd", lambda e: e.engine_nop(),
         waits=[(sGP, sGP.v), (sOU[0], sOU[0].v), (sOU[1], sOU[1].v)])

    with nc.Block() as block:
        @block.sync
        def _(sync):
            for f in ops["sync"]:
                f(sync)

        @block.vector
        def _(vector):
            for f in ops["vector"]:
                f(vector)

        @block.scalar
        def _(scalar):
            for f in ops["scalar"]:
                f(scalar)

        @block.tensor
        def _(tensor):
            for f in ops["tensor"]:
                f(tensor)

        @block.gpsimd
        def _(gpsimd):
            for f in ops["gpsimd"]:
                f(gpsimd)

    return nc


def prep_inputs(L0_attn_x, L0_attn_A_log, L0_attn_D, L0_attn_W_xproj,
                L0_ffn_wg, L0_ffn_W, L0_ffn_b):
    wxp = np.ascontiguousarray(
        np.asarray(L0_attn_W_xproj[:2048], np.float32).T.reshape(16, 128, 2048))
    wbc = np.ascontiguousarray(
        np.asarray(L0_attn_W_xproj[2048:2080], np.float32).T.reshape(16, 128, 32))
    alog = np.ascontiguousarray(
        np.asarray(L0_attn_A_log, np.float32).reshape(16, 128, 16)
        .transpose(1, 0, 2).reshape(128, 256))
    dvec = np.ascontiguousarray(np.asarray(L0_attn_D, np.float32).reshape(16, 128).T)
    wg = np.ascontiguousarray(
        np.asarray(L0_ffn_wg, np.float32).reshape(16, 128, 8)
        .transpose(1, 0, 2).reshape(128, 128))
    w5 = np.ascontiguousarray(
        np.asarray(L0_ffn_W, np.float32).reshape(8, 16, 128, 16, 128)
        .transpose(1, 0, 4, 3, 2).reshape(16, 8, 128, 2048))
    bias = np.ascontiguousarray(np.asarray(L0_ffn_b, np.float32))
    xs = [np.ascontiguousarray(np.asarray(L0_attn_x[i], np.float32)) for i in range(8)]
    common = {"wxp": wxp, "wbc": wbc, "alog": alog, "dvec": dvec,
              "wg": wg, "w5": w5, "bias": bias}
    return xs, common


def kernel(L0_attn_x, L0_attn_A_log, L0_attn_D, L0_attn_W_xproj,
           L0_ffn_wg, L0_ffn_W, L0_ffn_b):
    xs, common = prep_inputs(L0_attn_x, L0_attn_A_log, L0_attn_D, L0_attn_W_xproj,
                             L0_ffn_wg, L0_ffn_W, L0_ffn_b)
    nc = build_nc()
    in_maps = [dict(common, x=xs[i]) for i in range(8)]
    res = run_bass_kernel_spmd(nc, in_maps, core_ids=list(range(8)))
    out = np.stack([np.asarray(res.results[i]["y"], np.float32) for i in range(8)], axis=0)
    return out


if __name__ == "__main__":
    nc = build_nc()
    print("built ok, instructions:", nc.n_instructions())


# version 24 (final)
# speedup vs baseline: 1.3924x; 1.0030x over previous
import numpy as np
import concourse.bass as bass
import concourse.mybir as mybir
from concourse.bass_utils import run_bass_kernel_spmd

F32 = mybir.dt.float32
BF16 = mybir.dt.bfloat16
Alu = mybir.AluOpType
Act = mybir.ActivationFunctionType
AX = mybir.AxisListType.X

DM = 2048
LSEQ = 2048
NST = 16
NE = 8
NJ = 16
TS = 256
NCH = LSEQ // TS
TBT = 512
NTBG = LSEQ // TBT
EPS = 1e-5
RN = 1.0 / DM


class Sem:
    def __init__(self, nc, name):
        self._cm = nc.semaphore(name)
        self.h = self._cm.__enter__()
        self.v = 0


def build_nc():
    nc = bass.Bass("TRN2", target_bir_lowering=False)

    x_h = nc.dram_tensor("x", [LSEQ, DM], F32, kind="ExternalInput").ap()
    wxp_h = nc.dram_tensor("wxp", [16, 128, 2048], F32, kind="ExternalInput").ap()
    wbc_h = nc.dram_tensor("wbc", [16, 128, 32], F32, kind="ExternalInput").ap()
    alog_h = nc.dram_tensor("alog", [128, 256], F32, kind="ExternalInput").ap()
    dvec_h = nc.dram_tensor("dvec", [128, 16], F32, kind="ExternalInput").ap()
    wg_h = nc.dram_tensor("wg", [128, 128], F32, kind="ExternalInput").ap()
    b_h = nc.dram_tensor("bias", [8, 2048], F32, kind="ExternalInput").ap()
    w5_h = nc.dram_tensor("w5", [16, 8, 128, 2048], F32, kind="ExternalInput").ap()
    y_h = nc.dram_tensor("y", [LSEQ, DM], F32, kind="ExternalOutput").ap()
    wxpbf_s = nc.dram_tensor("wxpbf", [16, 128, 2048], BF16, kind="Internal").ap()
    wbf_s = nc.dram_tensor("wbf", [16, 8, 128, 2048], BF16, kind="Internal").ap()
    xp_s = nc.dram_tensor("xps", [LSEQ, DM], F32, kind="Internal").ap()
    ln2_s = nc.dram_tensor("ln2s", [NCH, 128, NJ * TS], BF16, kind="Internal").ap()

    def sb(name, shape, dt=F32):
        return nc.alloc_sbuf_tensor(name, shape, dt).ap()

    A_raw = sb("A_raw", [128, 256])
    A_sb = sb("A_sb", [128, 256])
    D_sb = sb("D_sb", [128, 16])
    wg_bf = sb("wg_bf", [128, 128], BF16)
    wxpbc_bf = sb("wxpbc", [128, 512], BF16)
    identf = sb("identf", [128, 128])
    identb = sb("identb", [128, 128], BF16)
    carry = sb("carry", [128, 512])
    masks = sb("masks", [128, 128])
    gate = sb("gate", [128, 16])
    flagb = sb("flagb", [128, 16])
    tmp16 = sb("tmp16", [128, 16])
    stats = sb("stats", [128, 64])
    lg_sb = sb("lg_sb", [8, 256])
    bc_sb = sb("bc_sb", [32, 256])
    bb_sb = sb("bb_sb", [128, 512])
    cb_sb = sb("cb_sb", [128, 512])
    wf32 = sb("wf32", [128, NJ * (TS + 1)])
    consts = sb("consts", [128, 8])

    xbuf0 = sb("xbuf0", [128, 2 * DM])
    xbuf1 = sb("xbuf1", [128, 2 * DM])
    tmbuf = sb("tmbuf", [128, NJ * (TS + 1)])
    fmbuf = sb("fmbuf", [128, NJ * TS])
    ln1fb = sb("ln1fb", [128, NJ * TS], BF16)
    dtb = sb("dtb", [128, NJ * TS])
    dtxb = sb("dtxb", [128, NJ * TS])
    ybuf = sb("ybuf", [128, NJ * TS])
    abuf = sb("abuf", [128, NJ * (TS + 1)])
    ubuf = sb("ubuf", [128, NJ * (TS + 1)])
    hbuf = sb("hbuf", [128, NJ * (TS + 1)])
    wxpK = sb("wxpK", [128, 2 * 2048], BF16)
    masksT = wxpK[0:8, 0:2048]
    b_bf = wxpK[0:8, 2048:4096]

    xe = [dtb[:].bitcast(BF16), dtxb[:].bitcast(BF16),
          ybuf[:].bitcast(BF16), abuf[:].bitcast(BF16)[:, 0:8192]]
    ln2TBG = ubuf[:].bitcast(BF16)[:, 0:8192]
    outFM_lo = hbuf
    outFM_hi = fmbuf
    wstage = xbuf0
    outtile = [xbuf1[:][:, 0:DM], xbuf1[:][:, DM:2 * DM]]
    xptile = [tmbuf[:][:, 0:DM], tmbuf[:][:, DM:2 * DM]]
    otmp = [ln1fb[:].bitcast(F32)[:, 0:512], ln1fb[:].bitcast(F32)[:, 512:1024]]
    meb_bf = ln1fb[:][:, 2048:2560]

    def wslot(i):
        v = wstage[:].bitcast(BF16)
        return v[:, i * 2048:(i + 1) * 2048]

    def outFM(mc):
        t = outFM_lo if mc < 8 else outFM_hi
        m = mc % 8
        return t[:][:, m * 512:(m + 1) * 512]

    ps_t = nc.alloc_psum_tensor("ps", [128, 4096], F32)
    ps = ps_t.ap()
    psb = ps.bitcast(BF16)

    def bank(b, lo=0, hi=512, parts=128):
        return ps[0:parts, b * 512 + lo: b * 512 + hi]

    def bankb(b, lo=0, hi=1024, parts=128):
        return psb[0:parts, b * 1024 + lo: b * 1024 + hi]

    sLD = Sem(nc, "sLD")
    sGP = Sem(nc, "sGP")
    sPC = Sem(nc, "sPC")
    sIO = Sem(nc, "sIO")
    sWX = [Sem(nc, "sWX0"), Sem(nc, "sWX1")]
    sXL = [Sem(nc, "sXL0"), Sem(nc, "sXL1")]
    sL2 = Sem(nc, "sL2")
    sLB = Sem(nc, "sLB")
    sWB = [Sem(nc, "sWB0"), Sem(nc, "sWB1")]
    sXP = [Sem(nc, "sXP0"), Sem(nc, "sXP1")]
    sOU = [Sem(nc, "sOU0"), Sem(nc, "sOU1")]
    sV = Sem(nc, "sV")
    sS = Sem(nc, "sS")
    sT = Sem(nc, "sT")
    sG = Sem(nc, "sG")

    ops = {"sync": [], "vector": [], "scalar": [], "tensor": [], "gpsimd": []}

    def emit(engn, maker, waits=(), inc=None, amt=1):
        wl = tuple((s.h, v) for (s, v) in waits if v > 0)
        if inc is not None:
            inc.v += amt
        ih = None if inc is None else inc.h

        def f(e, wl=wl, maker=maker, ih=ih, amt=amt):
            for (h, v) in wl:
                e.wait_ge(h, v)
            ins = maker(e)
            if ih is not None:
                ins.then_inc(ih, amt)
        ops[engn].append(f)
        return None if inc is None else inc.v

    def dma(engn, out, in_, waits=(), sem=None):
        sem = sem if sem is not None else (sGP if engn == "gpsimd" else sLD)
        return emit(engn, lambda e, o=out, i=in_: e.dma_start(out=o, in_=i),
                    waits, inc=sem, amt=16)

    cp = {}

    CONST_VALS = [0.0, 1.0, -1.0, -0.5, EPS, RN]
    for ci, val in enumerate(CONST_VALS):
        emit("gpsimd", lambda e, ci=ci, val=val: e.memset(consts[:, ci:ci + 1], val))
        nc.const_aps.register(F32, val, consts[:, ci:ci + 1])
    emit("gpsimd", lambda e: e.memset(identf, 0.0))
    emit("gpsimd", lambda e: e.affine_select(
        out=identf, in_=identf, compare_op=Alu.not_equal, fill=1.0, base=0,
        pattern=[[-1, 128]], channel_multiplier=1))
    emit("gpsimd", lambda e: e.memset(identb, 0.0))
    emit("gpsimd", lambda e: e.affine_select(
        out=identb, in_=identb, compare_op=Alu.not_equal, fill=1.0, base=0,
        pattern=[[-1, 128]], channel_multiplier=1))
    emit("gpsimd", lambda e: e.memset(carry, 0.0))
    emit("gpsimd", lambda e: e.memset(abuf, 0.0))
    emit("gpsimd", lambda e: e.memset(flagb, 1.0), inc=sG)
    cp["gp_pro"] = sG.v

    cp["alog_ld"] = dma("sync", A_raw, alog_h, sem=sIO)
    cp["dvec_ld"] = dma("sync", D_sb, dvec_h, waits=[(sIO, cp["alog_ld"])], sem=sIO)
    v1 = dma("sync", wf32[:, 0:128], wg_h, waits=[(sIO, cp["dvec_ld"])], sem=sIO)
    v2 = dma("sync", wf32[:, 128:640].rearrange("p (a b) -> p a b", a=16),
             wbc_h.transpose([1, 0, 2]), waits=[(sIO, v1)], sem=sIO)
    cp["wg_ld"] = emit("gpsimd", lambda e: e.tensor_copy(out=wg_bf, in_=wf32[:, 0:128]),
                       waits=[(sIO, v1)], inc=sG)
    cp["wbc_ld"] = emit("gpsimd", lambda e: e.tensor_copy(out=wxpbc_bf, in_=wf32[:, 128:640]),
                        waits=[(sIO, v2)], inc=sG)

    cp["A_exp"] = emit("scalar", lambda e: e.activation(out=A_sb, in_=A_raw, func=Act.Exp),
                       waits=[(sIO, cp["alog_ld"])], inc=sS)
    cp["A_neg"] = emit("vector", lambda e: e.tensor_scalar_mul(out=A_sb, in0=A_sb, scalar1=-1.0),
                       waits=[(sS, cp["A_exp"])], inc=sV)

    pslot = wf32[:].bitcast(BF16)
    NPS = 8
    pu = [(wxp_h[k][:, hf * 1024:(hf + 1) * 1024],
           wxpbf_s[k][:, hf * 1024:(hf + 1) * 1024])
          for k in range(16) for hf in range(2)]
    pu += [(w5_h[mc, e][:, hf * 1024:(hf + 1) * 1024],
            wbf_s[mc, e][:, hf * 1024:(hf + 1) * 1024])
           for mc in range(16) for e in range(8) for hf in range(2)]
    ldv = {}
    stv = {}
    slotsem = [sPC, sLB, sWB[0], sWB[1], sXP[0], sXP[1], sOU[0], sOU[1]]
    for i, (psrc, pdst) in enumerate(pu):
        ssem = slotsem[i % NPS]
        slot = pslot[:, (i % NPS) * 1024:((i % NPS) + 1) * 1024]
        wts = [(ssem, stv[i - NPS])] if i - NPS >= 0 else []
        ldv[i] = dma("gpsimd", slot, psrc, waits=wts, sem=ssem)
        stv[i] = dma("gpsimd", pdst, slot, waits=[(ssem, ldv[i])], sem=ssem)
        if i == 31:
            cp["wxp_pc"] = emit(
                "gpsimd", lambda e: e.engine_nop(),
                waits=[(slotsem[s], stv[24 + s]) for s in range(NPS)], inc=sG)
    cp["w_pc"] = emit(
        "gpsimd", lambda e: e.engine_nop(),
        waits=[(slotsem[s], stv[len(pu) - NPS + s]) for s in range(NPS)], inc=sG)

    xbufs = [xbuf0, xbuf1]

    for c in range(NCH):
        xb = xbufs[c % 2][:]
        w_x = []
        if c >= 2:
            w_x = [(sV, cp[("xdoneV", c - 2)]), (sS, cp[("xdoneS", c - 2)]),
                   (sLD, cp[("xpst", c - 2, 1)])]
        for ts in range(2):
            g = 2 * c + ts
            cp[("xld", c, ts)] = dma(
                "sync", xb[:, ts * DM:(ts + 1) * DM], x_h[g * 128:(g + 1) * 128, :],
                waits=w_x, sem=sXL[ts])

        w_h = [(sT, cp[("xproj", c - 1)])] if c >= 1 else []
        for ts in range(2):
            xt = xb[:, ts * DM:(ts + 1) * DM]
            dummy = ln1fb[:][:, ts * DM:(ts + 1) * DM]
            sc = 16 * ts
            emit("scalar", lambda e, xt=xt, d=dummy, o=stats[:, sc:sc + 1]:
                 e.activation(out=d, in_=xt, func=Act.Copy, accum_out=o),
                 waits=[(sXL[ts], cp[("xld", c, ts)])] + w_h, inc=sS)
            cp[("sum", c, ts)] = emit(
                "scalar", lambda e, xt=xt, d=dummy, o=stats[:, sc + 1:sc + 2]:
                e.activation(out=d, in_=xt, func=Act.Square, accum_out=o), inc=sS)

        def stat_chain(base, sumkey, invkey, nmikey):
            for ts in range(2):
                sc = base + 16 * ts
                emit("vector", lambda e, sc=sc: e.tensor_scalar(
                    out=stats[:, sc + 2:sc + 3], in0=stats[:, sc:sc + 1],
                    scalar1=RN, scalar2=None, op0=Alu.mult),
                    waits=[(sS, cp[(sumkey, c, ts)])], inc=sV)
                emit("vector", lambda e, sc=sc: e.tensor_scalar(
                    out=stats[:, sc + 3:sc + 4], in0=stats[:, sc + 1:sc + 2],
                    scalar1=RN, scalar2=None, op0=Alu.mult), inc=sV)
            vA = sV.v
            for ts in range(2):
                sc = base + 16 * ts
                emit("vector", lambda e, sc=sc: e.tensor_mul(
                    out=stats[:, sc + 4:sc + 5], in0=stats[:, sc + 2:sc + 3],
                    in1=stats[:, sc + 2:sc + 3]),
                    waits=[(sV, vA)] if ts == 0 else (), inc=sV)
            vB = sV.v
            for ts in range(2):
                sc = base + 16 * ts
                emit("vector", lambda e, sc=sc: e.tensor_sub(
                    out=stats[:, sc + 5:sc + 6], in0=stats[:, sc + 3:sc + 4],
                    in1=stats[:, sc + 4:sc + 5]),
                    waits=[(sV, vB)] if ts == 0 else (), inc=sV)
            vC = sV.v
            for ts in range(2):
                sc = base + 16 * ts
                emit("scalar", lambda e, sc=sc: e.activation(
                    out=stats[:, sc + 6:sc + 7], in_=stats[:, sc + 5:sc + 6],
                    func=Act.Ln, bias=EPS),
                    waits=([(sV, vC), (sG, cp["gp_pro"])] if ts == 0 else ()), inc=sS)
            for ts in range(2):
                sc = base + 16 * ts
                cp[(invkey, c, ts)] = emit("scalar", lambda e, sc=sc: e.activation(
                    out=stats[:, sc + 7:sc + 8], in_=stats[:, sc + 6:sc + 7],
                    func=Act.Exp, scale=-0.5), inc=sS)
            for ts in range(2):
                sc = base + 16 * ts
                cp[(nmikey, c, ts)] = emit("vector", lambda e, sc=sc: e.scalar_tensor_tensor(
                    out=stats[:, sc + 8:sc + 9], in0=stats[:, sc + 2:sc + 3],
                    scalar=-1.0, in1=stats[:, sc + 7:sc + 8],
                    op0=Alu.mult, op1=Alu.mult),
                    waits=[(sS, cp[(invkey, c, 1)])] if ts == 0 else (), inc=sV)

        stat_chain(0, "sum", "inv", "nmi")
        for ts in range(2):
            sc = 16 * ts
            xt = xb[:, ts * DM:(ts + 1) * DM]
            wts = [(sV, cp[("nmi", c, ts)])]
            if ts == 0 and c >= 1:
                wts.append((sV, cp[("scanf", c - 1)]))
            cp[("ln1tm", c, ts)] = emit(
                "scalar", lambda e, xt=xt, sc=sc, o=tmbuf[:][:, ts * DM:(ts + 1) * DM]:
                e.activation(out=o, in_=xt, func=Act.Identity,
                             scale=stats[:, sc + 7:sc + 8], bias=stats[:, sc + 8:sc + 9]),
                waits=wts, inc=sS)

        for grp in range(8):
            ts, jq = grp // 4, grp % 4
            bk = 6 + grp % 2
            for k in range(4):
                j = jq * 4 + k
                src = tmbuf[:][:, ts * DM + j * 128: ts * DM + (j + 1) * 128]
                wts = []
                if k == 0:
                    wts.append((sS, cp[("ln1tm", c, ts)]))
                    if grp >= 2:
                        wts.append((sS, cp[("trev", c, grp - 2)]))
                    elif c >= 1:
                        wts.append((sS, cp[("spfree", c - 1, 4 + grp % 2)]))
                        wts.append((sS, cp[("spfree", c - 1, 6 + grp % 2)]))
                    if grp == 0:
                        wts.append((sG, cp["gp_pro"]))
                emit("tensor", lambda e, s=src, d=bank(bk, k * 128, (k + 1) * 128):
                     e.transpose(d, s, identf), waits=wts, inc=sT)
            vtr = sT.v
            o32 = fmbuf[:].rearrange("p (a b) -> p a b", a=NJ)[:, jq * 4:(jq + 1) * 4,
                                                              ts * 128:(ts + 1) * 128]
            ob = ln1fb[:].rearrange("p (a b) -> p a b", a=NJ)[:, jq * 4:(jq + 1) * 4,
                                                              ts * 128:(ts + 1) * 128]
            wev = [(sT, vtr)]
            if grp == 0 and c >= 1:
                wev.append((sL2, cp[("l2st", c - 1)]))
            emit("scalar", lambda e, o=o32, b2=bank(bk): e.copy(out=o, in_=b2),
                 waits=wev, inc=sS)
            cp[("trev", c, grp)] = emit(
                "scalar", lambda e, o=ob, b2=bank(bk): e.copy(out=o, in_=b2), inc=sS)
        cp[("ln1fm", c)] = sS.v

        wbc0 = [(sS, cp[("ln1fm", c)]), (sG, cp["wbc_ld"])]
        if c >= 1:
            wbc0.append((sV, cp[("ytrev", c - 1)]))
        for k in range(16):
            emit("tensor", lambda e, k=k: e.matmul(
                bank(2, 0, 256, parts=32),
                wxpbc_bf[:, k * 32:(k + 1) * 32],
                ln1fb[:][:, k * TS:(k + 1) * TS],
                start=(k == 0), stop=(k == 15)),
                waits=(wbc0 if k == 0 else ()), inc=sT)
        cp[("bcmm", c)] = sT.v
        cp[("bcev", c)] = emit("scalar",
                               lambda e: e.copy(out=bc_sb, in_=bank(2, 0, 256, parts=32)),
                               waits=[(sT, cp[("bcmm", c)])], inc=sS)

        def wxp_load(k):
            wts = [(sG, cp["wxp_pc"])]
            kk = ("xpK", c, k - 2) if k >= 2 else (("xpK", c - 1, 14 + k) if c >= 1 else None)
            if kk is not None:
                wts.append((sT, cp[kk]))
            cp[("wxk", c, k)] = dma(
                "sync", wxpK[:, (k % 2) * 2048:((k % 2) + 1) * 2048],
                wxpbf_s[k], waits=wts, sem=sWX[k % 2])

        wxp_load(0)
        wxp_load(1)
        for k in range(16):
            for mc in range(16):
                wts = []
                if mc == 0:
                    wts.append((sWX[k % 2], cp[("wxk", c, k)]))
                    if k == 0:
                        wts.append((sS, cp[("bcev", c)]))
                        if c >= 1:
                            wts.append((sS, cp[("spfree_all", c - 1)]))
                            wts.append((sV, cp[("msk", c - 1, 1)]))
                emit("tensor", lambda e, k=k, mc=mc: e.matmul(
                    ps[:, mc * 256:(mc + 1) * 256],
                    wxpK[:, (k % 2) * 2048 + mc * 128:(k % 2) * 2048 + (mc + 1) * 128],
                    ln1fb[:][:, k * TS:(k + 1) * TS],
                    start=(k == 0 and mc % 2 == 0), stop=(k == 15)),
                    waits=wts, inc=sT)
            cp[("xpK", c, k)] = sT.v
            if k + 2 < 16:
                wxp_load(k + 2)
        cp[("xproj", c)] = sT.v

        SLOT = lambda q, i: hbuf[:][:, q * 2048 + i * 256: q * 2048 + (i + 1) * 256]
        for p in range(8):
            q = p % 2
            mcs = (2 * p, 2 * p + 1)
            for i, mc in enumerate(mcs):
                wts = []
                if p == 0 and i == 0:
                    wts = [(sT, cp[("xproj", c)])]
                    if c >= 1:
                        wts.append((sV, cp[("w", c - 1, 15)]))
                if p >= 2 and i == 0:
                    wts.append((sV, cp[("dtadd", c, 2 * (p - 2) + 1)]))
                emit("scalar", lambda e, mc=mc, o=SLOT(q, 0 + i): e.activation(
                    out=o, in_=ps[:, mc * 256:(mc + 1) * 256], func=Act.Abs),
                    waits=wts, inc=sS)
            for i, mc in enumerate(mcs):
                emit("scalar", lambda e, mc=mc, o=SLOT(q, 2 + i): e.activation(
                    out=o, in_=ps[:, mc * 256:(mc + 1) * 256], func=Act.Relu), inc=sS)
            cp[("spfree", c, p)] = sS.v
            for i in range(2):
                emit("scalar", lambda e, i=i, q=q: e.activation(
                    out=SLOT(q, 4 + i), in_=SLOT(q, 0 + i), func=Act.Exp, scale=-1.0), inc=sS)
            for i in range(2):
                emit("scalar", lambda e, i=i, q=q: e.activation(
                    out=SLOT(q, 6 + i), in_=SLOT(q, 4 + i), func=Act.Ln, bias=1.0), inc=sS)
            vsp = sS.v
            for i, mc in enumerate(mcs):
                cp[("dtadd", c, mc)] = emit(
                    "vector", lambda e, mc=mc, i=i, q=q: e.tensor_add(
                        out=dtb[:][:, mc * 256:(mc + 1) * 256],
                        in0=SLOT(q, 6 + i), in1=SLOT(q, 2 + i)),
                    waits=[(sS, vsp)], inc=sV)
        cp[("spfree_all", c)] = sS.v
        cp[("dt", c)] = sV.v
        cp[("dtx", c)] = emit("vector", lambda e: e.tensor_mul(
            out=dtxb, in0=dtb, in1=fmbuf),
            waits=[(sV, cp[("dt", c)])], inc=sV)
        for j in range(NJ):
            wts = []
            if j == 0:
                wts = [(sV, cp[("dtx", c)]), (sIO, cp["dvec_ld"])]
                if c >= 1:
                    wts.append((sV, cp[("ytrev", c - 1)]))
                    wts.append(cp[("ygf", c - 1)])
            cp[("yini", c, j)] = emit("scalar", lambda e, j=j: e.activation(
                out=ybuf[:][:, j * TS:(j + 1) * TS],
                in_=fmbuf[:][:, j * TS:(j + 1) * TS],
                func=Act.Identity, scale=D_sb[:, j:j + 1]),
                waits=wts, inc=sS)
        cp[("yini", c)] = sS.v

        SW = TS + 1
        a_bufs = [abuf[:], tmbuf[:][:, 0:NJ * SW]]
        a3 = [b.rearrange("p (a b) -> p a b", a=NJ) for b in a_bufs]
        GPY = c >= 3
        YGP = False
        u_bufs = [ubuf[:][:, 0:NJ * SW],
                  (wf32[:][:, 0:NJ * SW] if GPY else ubuf[:][:, 0:NJ * SW])]
        u3p = [b.rearrange("p (a b) -> p a b", a=NJ) for b in u_bufs]
        h3 = hbuf[:][:, 0:NJ * SW].rearrange("p (a b) -> p a b", a=NJ)
        d3 = lambda ap: ap.rearrange("p (a b) -> p a b", a=NJ)
        hblk = h3[:, :, 1:SW]


        def selmm(n):
            par = n % 2
            wts = [(sS, cp[("bcev", c)]), (sS, cp[("spfree", c, par)])]
            if n >= 2:
                wts.append((sS, cp[("bcbev", c, n - 2)]))
            emit("tensor", lambda e, n=n, par=par: e.matmul(
                bank(par, 0, 256),
                identf[0:32, n:n + 1].broadcast_to([32, 128]), bc_sb,
                start=True, stop=False), waits=wts, inc=sT)
            cp[("selmm", c, n)] = emit("tensor", lambda e, n=n, par=par: e.matmul(
                bank(par, 256, 512),
                identf[0:32, (n + 16):(n + 17)].broadcast_to([32, 128]), bc_sb,
                start=False, stop=True), inc=sT)

        def bcbevac(n):
            par = n % 2
            bb = bb_sb[:, par * 256:(par + 1) * 256]
            cb = cb_sb[:, par * 256:(par + 1) * 256]
            wev = [(sT, cp[("selmm", c, n)])]
            if n >= 2:
                wev.append((sV, cp[("u", c, n - 2)]))
                wev.append((sV, cp[("w", c, n - 2)]))
            emit("scalar", lambda e, bb=bb, par=par: e.copy(out=bb, in_=bank(par, 0, 256)),
                 waits=wev, inc=sS)
            cp[("bcbev", c, n)] = emit(
                "scalar", lambda e, cb=cb, par=par: e.copy(out=cb, in_=bank(par, 256, 512)),
                inc=sS)

        def agen(n):
            par = n % 2
            ab = a_bufs[par]
            for j in range(NJ):
                wts = []
                if j == 0:
                    wts = [(sV, cp[("dt", c)])]
                    if n >= 2:
                        wts.append((sV, cp[("scans", c, n - 2)]))
                    elif c >= 1:
                        wts.append((sV, cp[("scans", c - 1, 14 + n)]))
                    if par == 1:
                        wts.append((sT, cp[("bcmm", c)]))
                cp[("a", c, n, j)] = emit("scalar", lambda e, j=j, n=n, ab=ab: e.activation(
                    out=ab[:, j * SW + 1:(j + 1) * SW],
                    in_=dtb[:][:, j * TS:(j + 1) * TS],
                    func=Act.Exp, scale=A_sb[:, (j * NST + n):(j * NST + n + 1)]),
                    waits=wts, inc=sS)
            cp[("a", c, n)] = sS.v

        def v_steps(n):
            par = n % 2
            bb = bb_sb[:, par * 256:(par + 1) * 256]
            cb = cb_sb[:, par * 256:(par + 1) * 256]
            ab = a_bufs[par]
            u3 = u3p[par]
            ublk = u3[:, :, 1:SW]
            wts = [(sS, cp[("bcbev", c, n)])]
            if n == 0:
                wts.append((sV, cp[("dtx", c)]))
            if GPY and c == 3 and n <= 1 and par == 1:
                wts.append((sG, cp["w_pc"]))
            ykey = ("yg", c, n - 2) if n >= 2 else (("yg", c - 1, 14 + n) if c >= 1 else None)
            if ykey is not None and ykey in cp:
                wts.append(cp[ykey])
            emit("vector", lambda e, bb=bb, ublk=ublk: e.tensor_tensor(
                out=ublk, in0=d3(dtxb[:]),
                in1=bb.unsqueeze(1).broadcast_to([128, NJ, TS]), op=Alu.mult),
                waits=wts, inc=sV)
            emit("vector", lambda e, ab=ab: e.memset(
                ab.rearrange("p (a b) -> p a b", a=NJ)[:, :, 0:1], 0.0),
                waits=([(sT, cp[("bcmm", c)])] if n == 1 else ()), inc=sV)
            rpar = ((c + 1) % 2) * 256
            cp[("u", c, n)] = emit("vector", lambda e, n=n, rpar=rpar, u3=u3: e.tensor_copy(
                out=u3[:, :, 0:1],
                in_=carry[:, rpar + n:rpar + 256:NST].unsqueeze(2)),
                waits=([(sG, cp["gp_pro"])] if (c == 0 and n == 0) else ()), inc=sV)
            wts = [(sS, cp[("a", c, n)]), (sV, cp[("u", c, n)])]
            if c == 0 and n == 0:
                wts.append((sG, cp["gp_pro"]))
            cp[("scan", c, n)] = emit(
                "vector", lambda e, ab=ab, ub=u_bufs[par]: e.tensor_tensor_scan(
                    out=hbuf[:][:, 0:NJ * SW], data0=ab[:, 0:NJ * SW],
                    data1=ub, initial=0.0,
                    op0=Alu.mult, op1=Alu.add),
                waits=wts, inc=sV)
            cp[("scans", c, n)] = sV.v
            wpar = (c % 2) * 256
            cp[("cry", c, n)] = emit("vector", lambda e, n=n, wpar=wpar: e.tensor_copy(
                out=carry[:, wpar + n:wpar + 256:NST].unsqueeze(2),
                in_=h3[:, :, 256:257]),
                waits=[(sV, cp[("scans", c, n)])], inc=sV)
            cp[("w", c, n)] = emit("vector", lambda e, cb=cb, ublk=ublk: e.tensor_tensor(
                out=ublk, in0=hblk,
                in1=cb.unsqueeze(1).broadcast_to([128, NJ, TS]), op=Alu.mult),
                waits=[(sV, cp[("scans", c, n)])], inc=sV)
            wts = [(sV, cp[("w", c, n)])]
            if n == 0:
                wts.append((sS, cp[("yini", c)]))
            if GPY and YGP:
                cp[("yg", c, n)] = (sG, emit("gpsimd", lambda e, ublk=ublk: e.tensor_tensor(
                    out=d3(ybuf[:]), in0=d3(ybuf[:]), in1=ublk, op=Alu.add),
                    waits=wts, inc=sG))
            else:
                cp[("yg", c, n)] = (sV, emit("vector", lambda e, ublk=ublk: e.tensor_tensor(
                    out=d3(ybuf[:]), in0=d3(ybuf[:]), in1=ublk, op=Alu.add),
                    waits=wts, inc=sV))

        selmm(0)
        agen(0)
        bcbevac(0)
        selmm(1)
        agen(1)
        for n in range(NST):
            if n + 2 < NST:
                selmm(n + 2)
            v_steps(n)
            if n + 1 < NST:
                bcbevac(n + 1)
            if n + 2 < NST:
                agen(n + 2)
        cp[("scan_done", c)] = cp[("scans", c, 15)]
        cp[("scanf", c)] = cp[("scans", c, 15)]
        cp[("ygf", c)] = cp[("yg", c, 15)]

        for grp in range(8):
            ts, jq = grp // 4, grp % 4
            bk = 2 + grp % 2
            for k in range(4):
                j = jq * 4 + k
                src = ybuf[:][:, j * TS + ts * 128: j * TS + (ts + 1) * 128]
                wts = []
                if k == 0:
                    wts.append(cp[("ygf", c)])
                    wts.append((sS, cp[("spfree_all", c)]))
                    if grp >= 2:
                        wts.append((sV, cp[("ytr", c, grp - 2)]))
                emit("tensor", lambda e, s=src, d=bank(bk, k * 128, (k + 1) * 128):
                     e.transpose(d, s, identf), waits=wts, inc=sT)
            vtr = sT.v
            xs = xb[:, ts * DM + jq * 512: ts * DM + (jq + 1) * 512]
            cp[("ytr", c, grp)] = emit("vector", lambda e, b2=bank(bk), xs=xs:
                                       e.tensor_add(out=xs, in0=b2, in1=xs),
                                       waits=[(sT, vtr)], inc=sV)
        cp[("ytrev", c)] = sV.v
        cp[("xdoneV", c)] = sV.v
        for ts in range(2):
            cp[("xpst", c, ts)] = dma("sync", xp_s[(2 * c + ts) * 128:(2 * c + ts + 1) * 128, :],
                                      xb[:, ts * DM:(ts + 1) * DM],
                                      waits=[(sV, cp[("ytrev", c)])])
        ln2tm = fmbuf[:].bitcast(BF16)[:, 0:2 * DM]
        for ts in range(2):
            xpt = xb[:, ts * DM:(ts + 1) * DM]
            dummy = ln1fb[:][:, ts * DM:(ts + 1) * DM]
            sc = 32 + 16 * ts
            emit("scalar", lambda e, xt=xpt, d=dummy, o=stats[:, sc:sc + 1]:
                 e.activation(out=d, in_=xt, func=Act.Copy, accum_out=o),
                 waits=[(sV, cp[("ytrev", c)])] if ts == 0 else (), inc=sS)
            cp[("sum2", c, ts)] = emit(
                "scalar", lambda e, xt=xpt, d=dummy, o=stats[:, sc + 1:sc + 2]:
                e.activation(out=d, in_=xt, func=Act.Square, accum_out=o), inc=sS)
        stat_chain(32, "sum2", "inv2", "nmi2")
        for ts in range(2):
            sc = 32 + 16 * ts
            xpt = xb[:, ts * DM:(ts + 1) * DM]
            cp[("ln2tm", c, ts)] = emit(
                "scalar", lambda e, xt=xpt, sc=sc, o=ln2tm[:, ts * DM:(ts + 1) * DM]:
                e.activation(out=o, in_=xt, func=Act.Identity,
                             scale=stats[:, sc + 7:sc + 8], bias=stats[:, sc + 8:sc + 9]),
                waits=[(sV, cp[("nmi2", c, ts)])], inc=sS)
        ln2fm = fmbuf[:].bitcast(BF16)[:, 2 * DM:2 * DM + NJ * TS]
        for grp in range(4):
            ts, jh = grp // 2, grp % 2
            bk = 2 + grp % 2
            for k in range(8):
                j = jh * 8 + k
                src = ln2tm[:, ts * DM + j * 128: ts * DM + (j + 1) * 128]
                wts = []
                if k == 0:
                    wts.append((sS, cp[("ln2tm", c, ts)]))
                    wts.append((sV, cp[("ytrev", c)]))
                    if grp >= 2:
                        wts.append((sS, cp[("l2ev", c, grp - 2)]))
                emit("tensor", lambda e, s=src, d=bankb(bk, k * 128, (k + 1) * 128):
                     e.transpose(d, s, identb), waits=wts, inc=sT)
            vtr = sT.v
            ofm = ln2fm.rearrange("p (a b) -> p a b", a=NJ)[:, jh * 8:(jh + 1) * 8,
                                                            ts * 128:(ts + 1) * 128]
            cp[("l2ev", c, grp)] = emit(
                "scalar", lambda e, o=ofm, b2=bankb(bk): e.copy(out=o, in_=b2),
                waits=[(sT, vtr)], inc=sS)
        cp[("ln2fm", c)] = sS.v
        cp[("l2st", c)] = dma("sync", ln2_s[c], ln2fm, waits=[(sS, cp[("ln2fm", c)])],
                              sem=sL2)
        cp[("lntm_free", c)] = sS.v

        for k in range(16):
            wts = []
            if k == 0:
                wts = [(sS, cp[("ln2fm", c)]), (sG, cp["wg_ld"])]
                if c >= 1:
                    wts.append((sS, cp[("lgev", c - 1)]))
            emit("tensor", lambda e, k=k: e.matmul(
                bank(2, 0, 256, parts=8), wg_bf[:, k * 8:(k + 1) * 8],
                ln2fm[:, k * TS:(k + 1) * TS],
                start=(k == 0), stop=(k == 15)), waits=wts, inc=sT)
        vlg = sT.v
        cp[("lgev", c)] = emit("scalar",
                               lambda e: e.copy(out=lg_sb, in_=bank(2, 0, 256, parts=8)),
                               waits=[(sT, vlg)], inc=sS)
        for ts in range(2):
            g = 2 * c + ts
            lgT = bank(3, ts * 8, (ts + 1) * 8)
            wts = [(sS, cp[("lgev", c)])]
            if ts == 1:
                wts.append((sS, cp[("vex", c, 0)]))
                wts.append((sV, cp[("msk", c, 0)]))
            elif c >= 1:
                wts.append((sS, cp[("vex", c - 1, 1)]))
                wts.append((sV, cp[("msk", c - 1, 1)]))
            emit("tensor", lambda e, s=lg_sb[:, ts * 128:(ts + 1) * 128], d=lgT:
                 e.transpose(d, s, identf[0:8, 0:8]), waits=wts, inc=sT)
            vt = sT.v
            nmcol = stats[:, 60:61]
            secol = stats[:, 61:62]
            e8 = stats[:, 48:56]
            vnm = emit("vector", lambda e, lgT=lgT, o=nmcol: e.reduce_max(
                out=o, in_=lgT, axis=AX, negate=True),
                waits=[(sT, vt)], inc=sV)
            cp[("vex", c, ts)] = emit(
                "scalar", lambda e, lgT=lgT, o=e8, nm=nmcol, se=secol: e.activation(
                    out=o, in_=lgT, func=Act.Exp, bias=nm, accum_out=se),
                waits=[(sV, vnm)], inc=sS)
            emit("vector", lambda e, g=g, se=secol: e.reciprocal(
                out=gate[:, g:g + 1], in_=se), waits=[(sS, cp[("vex", c, ts)])], inc=sV)
            cp[("msk", c, ts)] = emit("vector", lambda e, g=g, lgT=lgT, nm=nmcol:
                                      e.tensor_scalar(
                                          out=masks[:, g * 8:(g + 1) * 8], in0=lgT,
                                          scalar1=nm, scalar2=0.0,
                                          op0=Alu.add, op1=Alu.is_ge),
                                      waits=[(sV, vnm)], inc=sV)
        cp[("xdoneS", c)] = sS.v

    vprev = cp[("msk", NCH - 1, 1)]
    for e_ in range(NE):
        mslice = masks[:, e_:128:8]
        v1 = emit("vector", lambda e, m=mslice: e.tensor_mul(out=m, in0=m, in1=flagb),
                  waits=[(sV, vprev), (sG, cp["gp_pro"])], inc=sV)
        v2 = emit("vector", lambda e, m=mslice: e.tensor_scalar(
            out=tmp16, in0=m, scalar1=-1.0, scalar2=1.0, op0=Alu.mult, op1=Alu.add),
            waits=[(sV, v1)], inc=sV)
        vprev = emit("vector", lambda e: e.tensor_mul(out=flagb, in0=flagb, in1=tmp16),
                     waits=[(sV, v2)], inc=sV)
    cp["tiebreak"] = vprev
    for grp in range(4):
        bk = 6 + grp % 2
        for k in range(4):
            tsg = grp * 4 + k
            src = masks[:, tsg * 8:(tsg + 1) * 8]
            wts = []
            if k == 0:
                wts.append((sV, cp["tiebreak"]))
                wts.append((sS, cp[("vex", NCH - 1, 1)]))
                if grp >= 2:
                    wts.append((sS, cp[("mtev", grp - 2)]))
            emit("tensor", lambda e, s=src, d=bank(bk, k * 128, (k + 1) * 128, parts=8):
                 e.transpose(d, s, identf), waits=wts, inc=sT)
        vtr = sT.v
        cp[("mtev", grp)] = emit(
            "scalar", lambda e, grp=grp, bk=bk: e.copy(
                out=masksT[:, grp * 512:(grp + 1) * 512],
                in_=bank(bk, 0, 512, parts=8)),
            waits=[(sT, vtr)], inc=sS)
    cp["masksT"] = sS.v
    cp["A_end_V"] = sV.v
    cp["A_end_S"] = sS.v
    cp["A_end_T"] = sT.v
    cp["A_end_G"] = sG.v

    def _prev_pair(tb, eh, mc, hp):
        idx = (((tb * 2 + eh) * 16 + mc) * 2 + hp) - 2
        if idx < 0:
            return None
        hpb = idx % 2
        mcb = (idx // 2) % 16
        ehb = (idx // 32) % 2
        tbb = idx // 64
        return (tbb, ehb, mcb, hpb)

    def _prev_unit(tb, eh, mc, e4, back):
        idx = ((tb * 2 + eh) * 16 + mc) * 4 + e4 - (2 if back == 2 else 4)
        if idx < 0:
            return None
        e4b = idx % 4
        mcb = (idx // 4) % 16
        ehb = (idx // 64) % 2
        tbb = idx // 128
        return (tbb, ehb, mcb, e4b)

    vb = dma("sync", wf32[0:8, 2048:4096], b_h,
             waits=[(sG, cp["w_pc"]), (sT, cp["A_end_T"]), (sV, cp["A_end_V"]),
                    (sG, cp["A_end_G"]), (sS, cp["A_end_S"])])
    cp["b_ld"] = emit("gpsimd", lambda e: e.tensor_copy(out=b_bf, in_=wf32[0:8, 2048:4096]),
                      waits=[(sLD, vb)], inc=sG)
    NKC = 16
    last_meb_ev = 0
    last_xe = 0
    for tb in range(NTBG):
        wts = [(sV, cp["A_end_V"]), (sS, cp["A_end_S"]), (sT, cp["A_end_T"]),
               (sG, cp["A_end_G"])]
        if tb >= 1:
            wts.append((sT, cp[("mmall", tb - 1)]))
        for hf in range(2):
            cp[("l2ld", tb, hf)] = dma(
                "sync", ln2TBG[:, hf * 4096:(hf + 1) * 4096], ln2_s[tb * 2 + hf],
                waits=(wts if hf == 0 else []), sem=sLB)
        for eh in range(2):
            for e4 in range(4):
                ex = eh * 4 + e4
                wts = [(sS, cp["masksT"])]
                if last_meb_ev:
                    wts.append((sS, last_meb_ev))
                vmm = emit("tensor", lambda e, ex=ex, tb=tb: e.matmul(
                    bank(2), identb[0:8, ex:ex + 1].broadcast_to([8, 128]),
                    masksT[:, tb * 512:(tb + 1) * 512],
                    start=True, stop=True), waits=wts, inc=sT)
                wev = [(sT, vmm)]
                if last_xe:
                    wev.append((sV, last_xe))
                last_meb_ev = emit("scalar", lambda e: e.copy(out=meb_bf, in_=bank(2)),
                                   waits=wev, inc=sS)
                wxe = [(sS, last_meb_ev), (sLB, cp[("l2ld", tb, 1)])]
                if tb > 0 or eh > 0:
                    prev = ("mmeh", tb, eh - 1) if eh > 0 else ("mmeh", tb - 1, 1)
                    wxe.append((sT, cp[prev]))
                src0 = ln2TBG.rearrange("p (b a c) -> p a b c", b=2, a=NJ)
                src1 = meb_bf.rearrange("p (b c) -> p b c", b=2).unsqueeze(1)\
                    .broadcast_to([128, NJ, 2, 256])
                dst = xe[e4].rearrange("p (a b c) -> p a b c", a=NJ, b=2)
                last_xe = emit(
                    "vector", lambda e, d=dst, s0=src0, s1=src1: e.tensor_tensor(
                        out=d, in0=s0, in1=s1, op=Alu.mult),
                    waits=wxe, inc=sV)
                cp[("xe", tb, eh, e4)] = last_xe
            for mc in range(16):
                wsl4 = wstage[:].bitcast(BF16).rearrange("p (a b) -> p a b", a=4)
                for hp in range(2):
                    wts = [(sG, cp["w_pc"])]
                    pkey = ("wld_prev", tb, eh, mc, hp)
                    prev = _prev_pair(tb, eh, mc, hp)
                    if prev is not None:
                        wts.append((sT, cp[("wsl",) + prev[:3] + (prev[3] * 2 + 1,)]))
                    vld = dma("sync", wsl4[:, hp * 2:(hp + 1) * 2],
                              wbf_s[mc, eh * 4 + hp * 2: eh * 4 + hp * 2 + 2]
                              .transpose([1, 0, 2]),
                              waits=wts, sem=sWB[hp])
                    cp[("wld", tb, eh, mc, hp * 2)] = vld
                    cp[("wld", tb, eh, mc, hp * 2 + 1)] = vld
                bk = mc % 2
                if eh == 0:
                    wts = [(sS, cp["masksT"]), (sG, cp["b_ld"])]
                    if mc >= 2:
                        kf = cp[("evB", tb, 0, mc - 2)]
                        wts.append((kf[0], kf[1]))
                    elif tb >= 1:
                        kf = cp[("evB", tb - 1, 1, 14 + mc)]
                        wts.append((kf[0], kf[1]))
                    emit("tensor", lambda e, mc=mc, tb=tb, bk=bk: e.matmul(
                        bank(bk), b_bf[:, mc * 128:(mc + 1) * 128],
                        masksT[:, tb * 512:(tb + 1) * 512],
                        start=True, stop=False), waits=wts, inc=sT)
                for e4 in range(4):
                    si = (mc * 4 + e4) % 4
                    for k in range(NKC):
                        wts = []
                        if k == 0:
                            wts.append((sWB[e4 // 2], cp[("wld", tb, eh, mc, e4)]))
                            wts.append((sV, cp[("xe", tb, eh, e4)]))
                            if e4 == 0 and eh == 1:
                                if mc >= 2:
                                    kf = cp[("evB", tb, 1, mc - 2)]
                                else:
                                    kf = cp[("evB", tb, 0, 14 + mc)]
                                wts.append((kf[0], kf[1]))
                        emit("tensor", lambda e, si=si, k=k, e4=e4, bk=bk, eh=eh:
                             e.matmul(
                                 bank(bk),
                                 wslot(si)[:, k * 128:(k + 1) * 128],
                                 xe[e4][:, k * 512:(k + 1) * 512],
                                 start=(eh == 1 and e4 == 0 and k == 0),
                                 stop=(e4 == 3 and k == NKC - 1)),
                             waits=wts, inc=sT)
                    cp[("wsl", tb, eh, mc, e4)] = sT.v
                vmm = sT.v
                if eh == 0:
                    cp[("evB", tb, 0, mc)] = (sS, emit(
                        "scalar", lambda e, mc=mc, bk=bk: e.copy(out=outFM(mc), in_=bank(bk)),
                        waits=[(sT, vmm)], inc=sS))
                else:
                    kf = cp[("evB", tb, 0, mc)]
                    cp[("evB", tb, 1, mc)] = (sV, emit(
                        "vector", lambda e, mc=mc, bk=bk: e.tensor_add(
                            out=outFM(mc), in0=outFM(mc), in1=bank(bk)),
                        waits=[(sT, vmm), (kf[0], kf[1])], inc=sV))
            cp[("mmeh", tb, eh)] = sT.v
        cp[("mmall", tb)] = sT.v
        for ts in range(4):
            gidx = tb * 4 + ts
            wxp = [(sLD, cp[("xpst", gidx // 2, gidx % 2)])]
            if tb == 0 and ts < 2:
                wxp += [(sV, cp["A_end_V"]), (sS, cp["A_end_S"])]
            if ts >= 2:
                wxp.append((sOU[ts % 2], cp[("outst", tb, ts - 2)]))
            elif tb >= 1:
                wxp.append((sOU[ts % 2], cp[("outst", tb - 1, ts + 2)]))
            cp[("xpl", tb, ts)] = dma(
                "gpsimd", xptile[ts % 2], xp_s[gidx * 128:(gidx + 1) * 128, :],
                waits=wxp, sem=sXP[ts % 2])
            for grp in range(4):
                bk = 3 + grp % 2
                for k in range(4):
                    mc = grp * 4 + k
                    src = outFM(mc)[:, ts * 128:(ts + 1) * 128]
                    wts = []
                    if k == 0:
                        kf = cp[("evB", tb, 1, grp * 4 + 3)]
                        wts.append((kf[0], kf[1]))
                        if grp >= 2:
                            wts.append((sV, cp[("otr", tb, ts, grp - 2)]))
                        elif ts >= 1:
                            wts.append((sV, cp[("otr", tb, ts - 1, grp + 2)]))
                        elif tb >= 1:
                            wts.append((sV, cp[("otr", tb - 1, 3, grp + 2)]))
                    emit("tensor", lambda e, s=src, d=bank(bk, k * 128, (k + 1) * 128):
                         e.transpose(d, s, identf), waits=wts, inc=sT)
                vtr = sT.v
                vsc = emit("scalar", lambda e, gidx=gidx, bk=bk, o=otmp[grp % 2]:
                           e.activation(out=o, in_=bank(bk), func=Act.Identity,
                                        scale=gate[:, gidx:gidx + 1]),
                           waits=[(sT, vtr)], inc=sS)
                wadd = [(sS, vsc), (sXP[ts % 2], cp[("xpl", tb, ts)])]
                if grp == 0:
                    if ts >= 2:
                        wadd.append((sOU[ts % 2], cp[("outst", tb, ts - 2)]))
                    elif tb >= 1:
                        wadd.append((sOU[ts % 2], cp[("outst", tb - 1, ts + 2)]))
                cp[("otr", tb, ts, grp)] = emit(
                    "vector", lambda e, ts=ts, grp=grp, o=otmp[grp % 2]: e.tensor_add(
                        out=outtile[ts % 2][:, grp * 512:(grp + 1) * 512],
                        in0=o, in1=xptile[ts % 2][:, grp * 512:(grp + 1) * 512]),
                    waits=wadd, inc=sV)
            cp[("outst", tb, ts)] = dma(
                "gpsimd", y_h[gidx * 128:(gidx + 1) * 128, :], outtile[ts % 2],
                waits=[(sV, cp[("otr", tb, ts, 3)])], sem=sOU[ts % 2])

    emit("gpsimd", lambda e: e.engine_nop(),
         waits=[(sGP, sGP.v), (sOU[0], sOU[0].v), (sOU[1], sOU[1].v)])

    with nc.Block() as block:
        @block.sync
        def _(sync):
            for f in ops["sync"]:
                f(sync)

        @block.vector
        def _(vector):
            for f in ops["vector"]:
                f(vector)

        @block.scalar
        def _(scalar):
            for f in ops["scalar"]:
                f(scalar)

        @block.tensor
        def _(tensor):
            for f in ops["tensor"]:
                f(tensor)

        @block.gpsimd
        def _(gpsimd):
            for f in ops["gpsimd"]:
                f(gpsimd)

    return nc


def prep_inputs(L0_attn_x, L0_attn_A_log, L0_attn_D, L0_attn_W_xproj,
                L0_ffn_wg, L0_ffn_W, L0_ffn_b):
    wxp = np.ascontiguousarray(
        np.asarray(L0_attn_W_xproj[:2048], np.float32).T.reshape(16, 128, 2048))
    wbc = np.ascontiguousarray(
        np.asarray(L0_attn_W_xproj[2048:2080], np.float32).T.reshape(16, 128, 32))
    alog = np.ascontiguousarray(
        np.asarray(L0_attn_A_log, np.float32).reshape(16, 128, 16)
        .transpose(1, 0, 2).reshape(128, 256))
    dvec = np.ascontiguousarray(np.asarray(L0_attn_D, np.float32).reshape(16, 128).T)
    wg = np.ascontiguousarray(
        np.asarray(L0_ffn_wg, np.float32).reshape(16, 128, 8)
        .transpose(1, 0, 2).reshape(128, 128))
    w5 = np.ascontiguousarray(
        np.asarray(L0_ffn_W, np.float32).reshape(8, 16, 128, 16, 128)
        .transpose(1, 0, 4, 3, 2).reshape(16, 8, 128, 2048))
    bias = np.ascontiguousarray(np.asarray(L0_ffn_b, np.float32))
    xs = [np.ascontiguousarray(np.asarray(L0_attn_x[i], np.float32)) for i in range(8)]
    common = {"wxp": wxp, "wbc": wbc, "alog": alog, "dvec": dvec,
              "wg": wg, "w5": w5, "bias": bias}
    return xs, common


def kernel(L0_attn_x, L0_attn_A_log, L0_attn_D, L0_attn_W_xproj,
           L0_ffn_wg, L0_ffn_W, L0_ffn_b):
    xs, common = prep_inputs(L0_attn_x, L0_attn_A_log, L0_attn_D, L0_attn_W_xproj,
                             L0_ffn_wg, L0_ffn_W, L0_ffn_b)
    nc = build_nc()
    in_maps = [dict(common, x=xs[i]) for i in range(8)]
    res = run_bass_kernel_spmd(nc, in_maps, core_ids=list(range(8)))
    out = np.stack([np.asarray(res.results[i]["y"], np.float32) for i in range(8)], axis=0)
    return out


if __name__ == "__main__":
    nc = build_nc()
    print("built ok, instructions:", nc.n_instructions())
